# version 09
# baseline (speedup 1.0000x reference)
import numpy as np
from contextlib import ExitStack

import concourse.bass as bass
import concourse.mybir as mybir
from concourse.bass_utils import run_bass_kernel_spmd

F32 = mybir.dt.float32
BF16 = mybir.dt.bfloat16
I32 = mybir.dt.int32
AF = mybir.ActivationFunctionType
ALU = mybir.AluOpType
AX = mybir.AxisListType

NCORES = 8
B_, L_, D_ = 2, 16384, 2048
CHUNK = 4096
TT = 512
NTILE = CHUNK // TT
ND = 16
NIT = 16
TC = 256
EPS = 1e-5
RS8192 = float(1.0 / np.sqrt(8192.0))
MAGIC = 0x5F3759DF
TILEW = 32768

ENGS = ("sync", "gpsimd", "vector", "scalar", "tensor")
OWNSEM = {"gpsimd": "sLg", "sync": "sSt", "tensor": "sTm", "vector": "sVe",
          "scalar": "sSc"}


def hadamard(n):
    H = np.array([[1.0]], dtype=np.float32)
    while H.shape[0] < n:
        H = np.block([[H, H], [H, -H]]).astype(np.float32)
    return H


class Builder:
    def __init__(self, nc, sem, es):
        self.nc = nc
        self.sem = sem
        self.es = es
        self.q = {e: [] for e in ENGS}
        self.cnt = {}
        self.bar = 0

    def slot_sem(self, name):
        if name not in self.sem:
            self.sem[name] = self.es.enter_context(self.nc.semaphore(name))
        return self.sem[name]

    def emit(self, eng, fn):
        self.q[eng].append(fn)

    def inc(self, s, k=1):
        self.cnt[s] = self.cnt.get(s, 0) + k
        return self.cnt[s]

    def cur(self, s):
        return self.cnt.get(s, 0)

    def wait(self, eng, w):
        if w is None or w[1] is None or w[1] <= 0:
            return
        s, v = w
        self.emit(eng, lambda e, s=s, v=v: e.wait_ge(self.sem[s], v))

    def waits(self, eng, ws):
        seen = {}
        for w in ws:
            if w is None or w[1] is None:
                continue
            seen[w[0]] = max(seen.get(w[0], 0), w[1])
        for s, v in seen.items():
            self.wait(eng, (s, v))

    def load(self, dst, src, waits=(), slot="g"):
        sn = "L_" + slot
        self.slot_sem(sn)
        self.waits("gpsimd", waits)
        t = self.inc(sn, 16)
        self.emit("gpsimd", lambda g, d=dst, s=src, sn=sn: g.dma_start(out=d, in_=s).then_inc(self.sem[sn], 16))
        return (sn, t)

    def store(self, dst, src, waits=(), slot="s"):
        sn = "S_" + slot
        self.slot_sem(sn)
        self.waits("sync", waits)
        t = self.inc(sn, 16)
        self.emit("sync", lambda e, d=dst, s=src, sn=sn: e.dma_start(out=d, in_=s).then_inc(self.sem[sn], 16))
        return (sn, t)

    def loadT(self, dst, src, waits=(), slot="t"):
        sn = "S_" + slot
        self.slot_sem(sn)
        self.waits("sync", waits)
        t = self.inc(sn, 16)
        self.emit("sync", lambda e, d=dst, s=src, sn=sn: e.dma_start(out=d, in_=s, transpose=True).then_inc(self.sem[sn], 16))
        return (sn, t)

    def mm(self, psum_fn, pairs, waits=(), start_first=True, stop_last=True):
        self.waits("tensor", waits)
        t = self.inc("sTm")
        n = len(pairs)
        for j, (lf, rf) in enumerate(pairs):
            first = (j == 0) and start_first
            last = j == n - 1
            def _m(e, lf=lf, rf=rf, first=first, last=last, pf=psum_fn, stop=stop_last):
                i = e.matmul(out=pf(), lhsT=lf(), rhs=rf(), start=first, stop=(last and stop))
                if last:
                    i.then_inc(self.sem["sTm"], 1)
            self.emit("tensor", _m)
        return ("sTm", t)

    def vop(self, fn, waits=(), inc=False, chain=False):
        self.waits("vector", waits)
        t = self.inc("sVe") if (inc or chain) else None
        def _f(v, fn=fn, t=t):
            i = fn(v)
            if t is not None:
                i.then_inc(self.sem["sVe"], 1)
        self.emit("vector", _f)
        if chain:
            self.wait("vector", ("sVe", t))
        return ("sVe", t) if (inc or chain) else None

    def cop(self, fn, waits=(), inc=False, chain=False):
        self.waits("scalar", waits)
        t = self.inc("sSc") if (inc or chain) else None
        def _f(c, fn=fn, t=t):
            i = fn(c)
            if t is not None:
                i.then_inc(self.sem["sSc"], 1)
        self.emit("scalar", _f)
        if chain:
            self.wait("scalar", ("sSc", t))
        return ("sSc", t) if (inc or chain) else None

    def barrier(self):
        self.bar += 1
        tgt = 5 * self.bar
        snap = {s: v for s, v in self.cnt.items()}
        for eng in ENGS:
            if eng == "gpsimd":
                own = [s for s in snap if s.startswith("L_") or s == "sLg"]
            elif eng == "sync":
                own = [s for s in snap if s.startswith("S_") or s == "sSt"]
            else:
                own = [OWNSEM[eng]]
            pairs = [(s, snap[s]) for s in own if snap.get(s, 0) > 0]
            def _f(e, pairs=pairs, tgt=tgt):
                for s, v in pairs:
                    e.wait_ge(self.sem[s], v)
                e.nop().then_inc(self.sem["sBar"], 1)
                e.wait_ge(self.sem["sBar"], tgt)
            self.emit(eng, _f)

    def rsqrt(self, buf, tmp, s_ap, waits=(), inc=False, n_iter=2, chain=True):
        bi = buf.bitcast(I32)
        si = s_ap.bitcast(I32)
        self.vop(lambda v: v.tensor_scalar(out=bi, in0=si, scalar1=1, scalar2=None, op0=ALU.logical_shift_right), waits=waits, chain=chain)
        self.vop(lambda v: v.tensor_scalar(out=bi, in0=bi, scalar1=-1, scalar2=None, op0=ALU.mult), chain=chain)
        self.vop(lambda v: v.tensor_scalar(out=bi, in0=bi, scalar1=MAGIC, scalar2=None, op0=ALU.add), chain=chain)
        w = None
        for it in range(n_iter):
            self.vop(lambda v: v.tensor_tensor(out=tmp, in0=buf, in1=buf, op=ALU.mult), chain=chain)
            self.vop(lambda v: v.tensor_tensor(out=tmp, in0=tmp, in1=s_ap, op=ALU.mult), chain=chain)
            self.vop(lambda v: v.tensor_scalar(out=tmp, in0=tmp, scalar1=-0.5, scalar2=1.5, op0=ALU.mult, op1=ALU.add), chain=chain)
            w = self.vop(lambda v: v.tensor_tensor(out=buf, in0=buf, in1=tmp, op=ALU.mult),
                         inc=(inc and it == n_iter - 1), chain=chain)
        return w


def build_program(debug_outputs=(), stage_limit=99.0, c_stats=True, detect_races=True, c_parts=3):
    nc = bass.Bass("TRN2", target_bir_lowering=False, num_devices=NCORES,
                   detect_race_conditions=detect_races)

    def dram(name, shape, dt, kind="Internal", addr_space=None):
        kw = {}
        if kind != "Internal":
            kw["kind"] = kind
        if name in debug_outputs:
            kw["kind"] = "ExternalOutput"
        if addr_space:
            kw["addr_space"] = addr_space
        return nc.dram_tensor(name, shape, dt, **kw)

    xTd = dram("xT", [128, 16 * 4099], F32, kind="ExternalInput").ap()
    xtokd = dram("x_tok", [CHUNK, D_], F32, kind="ExternalInput").ap()
    ind = dram("inT", [128, 65536], F32, kind="ExternalInput").ap()
    gated = dram("gateT", [128, 65536], F32, kind="ExternalInput").ap()
    outPd = dram("outP", [128, 65536], F32, kind="ExternalInput").ap()
    w1Td = dram("w1T", [128, 131072], F32, kind="ExternalInput").ap()
    w3Td = dram("w3T", [128, 131072], F32, kind="ExternalInput").ap()
    w2ad = dram("w2arr", [64, 262144], F32, kind="ExternalInput").ap()
    dtwd = [dram(f"dtw{h}", [128, 512], F32, kind="ExternalInput").ap() for h in range(2)]
    bpd = [dram(f"bp{h}", [32, 2048], F32, kind="ExternalInput").ap() for h in range(2)]
    ald = [dram(f"al{h}", [32, 2048], F32, kind="ExternalInput").ap() for h in range(2)]
    mskd = [dram(f"msk{h}", [32, 2048], F32, kind="ExternalInput").ap() for h in range(2)]
    convwd = dram("convw", [128, 128], F32, kind="ExternalInput").ap()
    convbd = dram("convb", [128, 32], F32, kind="ExternalInput").ap()
    cpd = dram("cp", [128, 32], F32, kind="ExternalInput").ap()
    dpd = dram("dp", [128, 32], F32, kind="ExternalInput").ap()
    dtbd = dram("dtb", [32, 2], F32, kind="ExternalInput").ap()
    h64d = dram("h64", [64, 64], F32, kind="ExternalInput").ap()
    h128d = dram("h128", [128, 128], F32, kind="ExternalInput").ap()
    maskd = dram("chmask", [128, 8], F32, kind="ExternalInput").ap()
    outd = dram("out", [CHUNK, D_], F32, kind="ExternalOutput").ap()

    sp_lnx = dram("sp_lnx", [128, 16 * 4099], BF16).ap()
    sp_y0 = dram("sp_y0", [128, 32 * CHUNK], BF16).ap()
    sp_ag = dram("sp_ag", [128, 32 * CHUNK], BF16).ap()
    sp_comb = dram("sp_comb", [128, 32 * CHUNK], BF16).ap()
    sp_mo1 = dram("sp_mo1", [CHUNK, 1024], BF16).ap()
    sp_x1 = dram("sp_x1", [CHUNK, D_], F32).ap()
    sp_lnx1 = dram("sp_lnx1", [CHUNK, D_], BF16).ap()
    sp_hid = dram("sp_hid", [128, 64 * CHUNK], BF16).ap()
    sp_ffn1 = dram("sp_ffn1", [CHUNK, 1024], BF16).ap()
    sp_st2 = dram("sp_st2", [CHUNK, 2], F32).ap()
    sp_st4 = dram("sp_st4", [CHUNK, 2], F32).ap()
    w2f = dram("w2f", [64, 262144], BF16).ap()
    cc_in = dram("cc_in", [128, 64], F32).ap()
    cc_out = dram("cc_out", [128 * NCORES, 64], F32, addr_space="Shared").ap()

    es = ExitStack()
    sem = {}
    for name in ("sLg", "sSt", "sCC", "sTm", "sVe", "sSc", "sBar"):
        sem[name] = es.enter_context(nc.semaphore(name))
    S = Builder(nc, sem, es)

    p_sin = es.enter_context(nc.sbuf_tensor("p_sin", [128, 32], F32))
    p_fcar = es.enter_context(nc.sbuf_tensor("p_fcar", [128, 32], F32))
    p_dcar = es.enter_context(nc.sbuf_tensor("p_dcar", [128, 32], F32))

    V = lambda fn, **kw: S.vop(fn, **kw)
    C = lambda fn, **kw: S.cop(fn, **kw)

    g1 = ExitStack()
    wA = g1.enter_context(nc.sbuf_tensor("wA", [128, TILEW], BF16))
    k_convw = g1.enter_context(nc.sbuf_tensor("k_convw", [128, 128], F32))
    k_convb = g1.enter_context(nc.sbuf_tensor("k_convb", [128, 32], F32))
    k_cp2 = g1.enter_context(nc.sbuf_tensor("k_cp2", [128, 32], F32))
    k_dp2 = g1.enter_context(nc.sbuf_tensor("k_dp2", [128, 32], F32))
    k_dtb = g1.enter_context(nc.sbuf_tensor("k_dtb", [32, 2], F32))
    k_mask = g1.enter_context(nc.sbuf_tensor("k_mask", [128, 8], F32))
    k_ones_bf = g1.enter_context(nc.sbuf_tensor("k_ones_bf", [128, 1], BF16))
    k_ones_row = g1.enter_context(nc.sbuf_tensor("k_ones_row", [1, 128], F32))
    k_dtw = [g1.enter_context(nc.sbuf_tensor(f"k_dtw{h}", [128, 512], BF16)) for h in range(2)]
    k_bp = [g1.enter_context(nc.sbuf_tensor(f"k_bp{h}", [32, 2048], BF16)) for h in range(2)]
    k_mdec = [g1.enter_context(nc.sbuf_tensor(f"k_mdec{h}", [32, 2048], BF16)) for h in range(2)]
    zerot = g1.enter_context(nc.sbuf_tensor("zerot", [128, TT], F32))
    cc_sb = g1.enter_context(nc.sbuf_tensor("cc_sb", [128, 64], F32))
    cc_g = g1.enter_context(nc.sbuf_tensor("cc_g", [128, 512], F32))
    si_t = g1.enter_context(nc.sbuf_tensor("si_t", [128, 32], F32))
    si_t2 = g1.enter_context(nc.sbuf_tensor("si_t2", [128, 32], F32))

    lws = []
    for i, (dst, srcap) in enumerate([(k_convw, convwd), (k_convb, convbd), (k_cp2, cpd),
                                      (k_dp2, dpd), (k_dtb, dtbd), (k_mask, maskd)]):
        lws.append(S.load(dst[:, :], srcap, slot=f"k{i}"))
    for h in range(2):
        lws.append(S.load(k_dtw[h][:, :], dtwd[h], slot=f"kdtw{h}"))
        lws.append(S.load(k_bp[h][:, :], bpd[h], slot=f"kbp{h}"))
    w_kload = lws[-1]
    for buf, val in [(zerot, 0.0), (p_fcar, 0.0), (p_dcar, 1.0), (p_sin, 0.0),
                     (k_ones_bf, 1.0), (k_ones_row, 1.0)]:
        S.emit("gpsimd", lambda g, b=buf, v=val: g.memset(b[:, :], v).then_inc(sem["sLg"], 1))
        S.inc("sLg", 1)
    w_memset = ("sLg", S.cur("sLg"))

    w_cp2 = C(lambda c: c.mul(out=k_cp2[:, :], in_=k_cp2[:, :], mul=0.5),
              waits=[lws[2], w_memset], inc=True)
    w_dp2 = C(lambda c: c.mul(out=k_dp2[:, :], in_=k_dp2[:, :], mul=0.25),
              waits=[lws[3]], inc=True)
    w_dtwh = []
    for h in range(2):
        wd = C(lambda c, h=h: c.mul(out=k_dtw[h][:, :], in_=k_dtw[h][:, :], mul=0.5),
               waits=[lws[6 + 2 * h]], inc=True)
        wb_ = C(lambda c, h=h: c.mul(out=k_bp[h][:, :], in_=k_bp[h][:, :], mul=0.5),
                waits=[lws[7 + 2 * h]], inc=True)
        w_dtwh.append((wd, wb_))

    sub = ExitStack()
    m_tmp = sub.enter_context(nc.sbuf_tensor("m_tmp", [32, 2048], F32))
    k_msk = sub.enter_context(nc.sbuf_tensor("k_msk", [32, 2048], F32))
    k_h64 = sub.enter_context(nc.sbuf_tensor("k_h64", [64, 64], BF16))
    fo_rhs = [sub.enter_context(nc.sbuf_tensor(f"fo_rhs{i}", [64, 2048], BF16)) for i in range(4)]
    fo_out = [sub.enter_context(nc.sbuf_tensor(f"fo_out{i}", [64, 2048], BF16)) for i in range(4)]
    pfold = [sub.enter_context(nc.psum_tensor(f"pfold{i}", [64, 512], F32)) for i in range(4)]

    w_h64 = S.load(k_h64[:, :], h64d, slot="h64")
    w_mdec = []
    prev_v = None
    for h in range(2):
        la = S.load(m_tmp[:, :], ald[h], waits=[prev_v] if prev_v else (), slot="mt")
        lm = S.load(k_msk[:, :], mskd[h], waits=[prev_v] if prev_v else (), slot="mk")
        we = C(lambda c: c.activation(out=m_tmp[:, :], in_=m_tmp[:, :], func=AF.Exp),
               waits=[la], inc=True)
        wv = V(lambda v, h=h: v.scalar_tensor_tensor(out=k_mdec[h][:, :], in0=m_tmp[:, :], scalar=-1.0, in1=k_msk[:, :], op0=ALU.mult, op1=ALU.mult),
               waits=[we, lm], inc=True)
        w_mdec.append(wv)
        prev_v = wv

    NFB = 4
    fo_sc = [None] * NFB
    fo_st = [None] * NFB
    fo_psc = [None] * 4
    for j4 in range(128):
        b4 = j4 % NFB
        lt = S.load(fo_rhs[b4][:, :], w2ad[:, 2048 * j4:2048 * (j4 + 1)],
                    waits=[w for w in (fo_sc[b4],) if w], slot=f"fo{b4}")
        wscs = None
        for jj in range(4):
            j = j4 * 4 + jj
            wm = S.mm(lambda jj=jj: pfold[jj][:, :],
                      [(lambda: k_h64[:, :], lambda b4=b4, jj=jj: fo_rhs[b4][:, jj * 512:(jj + 1) * 512])],
                      waits=[lt, w_h64 if j == 0 else None, fo_psc[jj]])
            wscs = C(lambda c, b4=b4, jj=jj: c.activation(out=fo_out[b4][:, jj * 512:(jj + 1) * 512], in_=pfold[jj][:, :], func=AF.Copy, scale=RS8192),
                     waits=[wm, fo_st[b4] if jj == 0 else None], inc=True)
            fo_psc[jj] = wscs
        st = S.store(w2f[:, 2048 * j4:2048 * (j4 + 1)], fo_out[b4][:, :], waits=[wscs], slot=f"w2f{b4}")
        fo_st[b4] = st
        fo_sc[b4] = st

    S.barrier()
    sub.close()

    a_x = g1.enter_context(nc.sbuf_tensor("a_x", [128, 16 * TT], BF16))
    a_ln = g1.enter_context(nc.sbuf_tensor("a_ln", [128, 16 * TT], BF16))
    a_X = [g1.enter_context(nc.sbuf_tensor(f"a_X{i}", [128, 16 * TT], BF16)) for i in range(2)]
    a_y0 = [g1.enter_context(nc.sbuf_tensor(f"a_y0{i}", [128, TT], BF16)) for i in range(2)]
    a_ag = [g1.enter_context(nc.sbuf_tensor(f"a_ag{i}", [128, TT], BF16)) for i in range(2)]
    a_pj = [g1.enter_context(nc.sbuf_tensor(f"a_pj{i}", [128, TT + 4], BF16)) for i in range(2)]
    a_acc = g1.enter_context(nc.sbuf_tensor("a_acc", [128, TT], BF16))
    a_tn = g1.enter_context(nc.sbuf_tensor("a_tn", [128, TT], BF16))
    a_t1 = g1.enter_context(nc.sbuf_tensor("a_t1", [128, TT], BF16))
    f_t = g1.enter_context(nc.sbuf_tensor("f_t", [128, TT], F32))
    f_dec = [g1.enter_context(nc.sbuf_tensor(f"f_dec{i}", [128, TT], F32)) for i in range(2)]
    f_drv = [g1.enter_context(nc.sbuf_tensor(f"f_drv{i}", [128, TT], F32)) for i in range(2)]
    f_st = [g1.enter_context(nc.sbuf_tensor(f"f_st{i}", [128, TT], F32)) for i in range(2)]
    f_ca = [g1.enter_context(nc.sbuf_tensor(f"f_ca{i}", [128, TT], F32)) for i in range(2)]
    f_sq = g1.enter_context(nc.sbuf_tensor("f_sq", [128, 4 * TT], BF16))
    d_zb = g1.enter_context(nc.sbuf_tensor("d_zb", [32, TT], F32))
    d_u = g1.enter_context(nc.sbuf_tensor("d_u", [32, TT], F32))
    d_y = g1.enter_context(nc.sbuf_tensor("d_y", [32, TT], F32))
    d_t = g1.enter_context(nc.sbuf_tensor("d_t", [32, TT], F32))
    d_bf = [g1.enter_context(nc.sbuf_tensor(f"d_bf{i}", [32, TT], BF16)) for i in range(2)]
    r_mean = g1.enter_context(nc.sbuf_tensor("r_mean", [1, TT], F32))
    r_msq = g1.enter_context(nc.sbuf_tensor("r_msq", [1, TT], F32))
    r_var = g1.enter_context(nc.sbuf_tensor("r_var", [1, TT], F32))
    r_r = g1.enter_context(nc.sbuf_tensor("r_r", [1, TT], F32))
    r_t = g1.enter_context(nc.sbuf_tensor("r_t", [1, TT], F32))
    stash = g1.enter_context(nc.sbuf_tensor("stash", [128, 48], BF16))
    pp = [g1.enter_context(nc.psum_tensor(f"pp{i}", [128, TT], F32)) for i in range(2)]
    pdt = g1.enter_context(nc.psum_tensor("pdt", [32, TT], F32))
    pdec = g1.enter_context(nc.psum_tensor("pdec", [128, TT], F32))
    pdrv = g1.enter_context(nc.psum_tensor("pdrv", [128, TT], F32))
    pst = g1.enter_context(nc.psum_tensor("pst", [128, TT], F32))
    psq = g1.enter_context(nc.psum_tensor("psq", [128, TT], F32))
    pbc = g1.enter_context(nc.psum_tensor("pbc", [128, TT], F32))

    S.emit("gpsimd", lambda g: g.memset(stash[:, :], 0.0).then_inc(sem["sLg"], 1))
    w_stash0 = ("sLg", S.inc("sLg", 1))

    def emit_stats(width, src_fn, extra_wait=None, nsrc=16):
        sl = slice(0, width)
        w_sum = S.mm(lambda: pst[0:1, sl],
                     [(lambda: k_ones_bf[:, :], (lambda dt=dt: src_fn(dt))) for dt in range(nsrc)],
                     waits=[stats_w[0], extra_wait, p2rel[0]])
        w_sumq = None
        for half in range(4):
            w_sql = None
            for i in range(4):
                dt = half * 4 + i
                w_sql = C(lambda c, dt=dt, i=i: c.activation(out=f_sq[:, i * TT:i * TT + width], in_=src_fn(dt), func=AF.Square),
                          waits=([stats_w[1], w_sumq, extra_wait] if i == 0 else ()), inc=(i == 3))
            w_sumq = S.mm(lambda: psq[0:1, sl],
                          [(lambda: k_ones_bf[:, :], (lambda i=i: f_sq[:, i * TT:i * TT + width])) for i in range(4)],
                          waits=[w_sql, stats_w[0] if half == 0 else None,
                                 bc_w[1] if half == 0 else None,
                                 p2rel[1] if half == 0 else None],
                          start_first=(half == 0), stop_last=(half == 3))
        ch_ = True
        V(lambda v: v.tensor_scalar(out=r_mean[:, sl], in0=pst[0:1, sl], scalar1=1.0 / D_, scalar2=None, op0=ALU.mult), waits=[w_sum], chain=ch_)
        V(lambda v: v.tensor_scalar(out=r_msq[:, sl], in0=psq[0:1, sl], scalar1=1.0 / D_, scalar2=None, op0=ALU.mult), waits=[w_sumq], chain=ch_)
        V(lambda v: v.tensor_tensor(out=r_t[:, sl], in0=r_mean[:, sl], in1=r_mean[:, sl], op=ALU.mult), chain=ch_)
        V(lambda v: v.tensor_tensor(out=r_var[:, sl], in0=r_msq[:, sl], in1=r_t[:, sl], op=ALU.subtract), chain=ch_)
        w_stat = V(lambda v: v.tensor_scalar(out=r_var[:, sl], in0=r_var[:, sl], scalar1=EPS, scalar2=None, op0=ALU.add), inc=True, chain=ch_)
        stats_w[0] = w_stat
        stats_w[1] = w_sumq
        S.rsqrt(r_r[:, sl], r_msq[:, sl], r_var[:, sl], chain=ch_)
        w_mr = V(lambda v: v.tensor_tensor(out=r_t[:, sl], in0=r_mean[:, sl], in1=r_r[:, sl], op=ALU.mult), inc=True, chain=ch_)
        w_bcr = S.mm(lambda: pbc[:, sl], [(lambda: k_ones_row[:, :], lambda: r_r[0:1, sl])],
                     waits=[w_mr, bc_w[0]])
        w_bcm = S.mm(lambda: psq[:, sl], [(lambda: k_ones_row[:, :], lambda: r_t[0:1, sl])],
                     waits=[w_mr, w_stat, bc_w[1]])
        return w_bcr, w_bcm

    stats_w = [w_memset, w_memset]
    bc_w = [None, None]
    p2rel = [None, None]

    def stage_A(h):
        nonlocal a_ln_w
        wtc = []
        for ckk in range(4):
            wtc.append(S.load(wA[:, ckk * 8192:(ckk + 1) * 8192],
                              ind[:, h * TILEW + ckk * 8192:h * TILEW + (ckk + 1) * 8192],
                              slot="wA"))
        mdec = k_mdec[h]
        kbp = k_bp[h]
        kdtw = k_dtw[h]

        pp_w = [None, None]
        st_w = [[], []]
        pdec_w = [None, None]
        pdrv_w = [None, None]
        pdt_w = [None]

        def lhsT_in(dt, it):
            return wA[:, (it * 16 + dt) * 128:(it * 16 + dt) * 128 + 128]

        if h == 0:
            S.emit("gpsimd", lambda g: g.memset(a_ln[:, :], 0.0).then_inc(sem["sLg"], 1))
            w_amem = ("sLg", S.inc("sLg", 1))
            a_ln_w.append(w_amem)
            xb3 = S.load(a_x[:, 0:64].rearrange("p (a b) -> p a b", a=16),
                         xTd.rearrange("p (a b) -> p a b", a=16)[:, :, 0:4],
                         waits=[w_memset], slot="ax")
            w_bcr, w_bcm = emit_stats(3, lambda dt: a_x[:, dt * 4:dt * 4 + 3], extra_wait=xb3)
            w_ln3 = None
            first_ln_waits = [w_amem]
            for dt in range(ND):
                V(lambda v, dt=dt: v.tensor_tensor(out=f_t[:, 0:3], in0=a_x[:, dt * 4:dt * 4 + 3], in1=pbc[:, 0:3], op=ALU.mult),
                  waits=([w_bcr, w_bcm] + first_ln_waits) if dt == 0 else (), chain=True)
                w_ln3 = V(lambda v, dt=dt: v.tensor_tensor(out=a_ln[:, dt * TT:dt * TT + 3], in0=f_t[:, 0:3], in1=psq[:, 0:3], op=ALU.subtract),
                          inc=(dt == ND - 1), chain=True)
            bc_w[0] = w_ln3
            bc_w[1] = w_ln3
            rst = S.store(sp_lnx.rearrange("p (a b) -> p a b", a=16)[:, :, 0:4],
                          a_ln[:, :].rearrange("p (a b) -> p a b", a=16)[:, :, 0:4], waits=[w_ln3], slot="slnx")
        else:
            xb3 = S.load(a_ln[:, :].rearrange("p (a b) -> p a b", a=16)[:, :, 0:4],
                         sp_lnx.rearrange("p (a b) -> p a b", a=16)[:, :, 0:4],
                         waits=a_ln_w, slot="aln")
            w_ln3 = xb3
        for it in range(NIT):
            w = S.mm(lambda: pp[0][:, 0:3],
                     [((lambda dt=dt, it=it: lhsT_in(dt, it)), (lambda dt=dt: a_ln[:, dt * TT:dt * TT + 3])) for dt in range(ND)],
                     waits=[w_ln3, wtc[it // 4], pp_w[0], w_stash0])
            pp_w[0] = V(lambda v, it=it: v.tensor_copy(out=stash[:, it * 3:it * 3 + 3], in_=pp[0][:, 0:3]),
                        waits=[w], inc=True)
        a_ln_w = [("sTm", S.cur("sTm"))] + ([rst] if h == 0 else [])
        if h == 0:
            a_x_w[:] = [w_ln3, ("sSc", S.cur("sSc")), ("sTm", S.cur("sTm"))]

        TST = [None] * NTILE

        def phase1(ti):
            nonlocal a_ln_w
            st = {}
            xb = ti % 2
            t0 = ti * TT
            if h == 0:
                xw = S.load(a_x[:, :].rearrange("p (a b) -> p a b", a=16),
                            xTd.rearrange("p (a b) -> p a b", a=16)[:, :, 3 + t0:3 + t0 + TT],
                            waits=a_x_w, slot="ax")
                w_bcr, w_bcm = emit_stats(TT, lambda dt: a_x[:, dt * TT:(dt + 1) * TT],
                                          extra_wait=xw)
                w_sq_end = ("sSc", S.cur("sSc"))
                w_stats_mm = ("sTm", S.cur("sTm"))
                w_lnx = None
                for dt in range(ND):
                    V(lambda v, dt=dt: v.tensor_tensor(out=f_t[:, :], in0=a_x[:, dt * TT:(dt + 1) * TT], in1=pbc[:, :], op=ALU.mult),
                      waits=([w_bcr, w_bcm] + a_ln_w) if dt == 0 else ())
                    w_lnx = V(lambda v, dt=dt: v.tensor_tensor(out=a_ln[:, dt * TT:(dt + 1) * TT], in0=f_t[:, :], in1=psq[:, :], op=ALU.subtract),
                              inc=(dt == ND - 1))
                bc_w[0] = w_lnx
                bc_w[1] = w_lnx

                a_x_w[:] = [w_lnx, w_sq_end, w_stats_mm]
                tst = S.store(sp_lnx.rearrange("p (a b) -> p a b", a=16)[:, :, 3 + t0:3 + t0 + TT],
                              a_ln[:, :].rearrange("p (a b) -> p a b", a=16), waits=[w_lnx], slot="slnx")
                a_ln_next = [tst]
            else:
                w_lnx = S.load(a_ln[:, :].rearrange("p (a b) -> p a b", a=16),
                               sp_lnx.rearrange("p (a b) -> p a b", a=16)[:, :, 3 + t0:3 + t0 + TT],
                               waits=a_ln_w, slot="aln")
                a_ln_next = []

            for it in range(NIT):
                pb = it % 2
                kt = h * 16 + it
                w_mm = S.mm(lambda pb=pb: pp[pb][:, :],
                            [((lambda dt=dt, it=it: lhsT_in(dt, it)),
                              (lambda dt=dt: a_ln[:, dt * TT:(dt + 1) * TT])) for dt in range(ND)],
                            waits=[w_lnx, wtc[it // 4] if ti == 0 else None, pp_w[pb]])
                w_pe = C(lambda c, pb=pb: c.activation(out=a_pj[pb][:, 4:TT + 4], in_=pp[pb][:, :], func=AF.Copy),
                         waits=[w_mm, pj_rel[pb]], inc=True)
                pp_w[pb] = w_pe
                V(lambda v, pb=pb, it=it: v.tensor_copy(out=a_pj[pb][:, 1:4], in_=stash[:, it * 3:it * 3 + 3]),
                  waits=[w_pe], chain=True)
                V(lambda v, pb=pb, kt=kt: v.tensor_scalar(out=a_acc[:, :], in0=a_pj[pb][:, 1:TT + 1],
                                                          scalar1=k_convw[:, 4 * kt:4 * kt + 1],
                                                          scalar2=k_convb[:, kt:kt + 1],
                                                          op0=ALU.mult, op1=ALU.add),
                  waits=[acc_rel[0]])
                V(lambda v, pb=pb, kt=kt: v.scalar_tensor_tensor(out=a_acc[:, :], in0=a_pj[pb][:, 2:TT + 2],
                                                                 scalar=k_convw[:, 4 * kt + 1:4 * kt + 2],
                                                                 in1=a_acc[:, :], op0=ALU.mult, op1=ALU.add))
                V(lambda v, pb=pb, kt=kt: v.scalar_tensor_tensor(out=a_acc[:, :], in0=a_pj[pb][:, 3:TT + 3],
                                                                 scalar=k_convw[:, 4 * kt + 2:4 * kt + 3],
                                                                 in1=a_acc[:, :], op0=ALU.mult, op1=ALU.add))
                w_acc = V(lambda v, pb=pb, kt=kt: v.scalar_tensor_tensor(out=a_acc[:, :], in0=a_pj[pb][:, 4:TT + 4],
                                                                         scalar=k_convw[:, 4 * kt + 3:4 * kt + 4],
                                                                         in1=a_acc[:, :], op0=ALU.mult, op1=ALU.add), inc=True)
                w_stash = V(lambda v, pb=pb, it=it: v.tensor_copy(out=stash[:, it * 3:it * 3 + 3], in_=a_pj[pb][:, TT + 1:TT + 4]), inc=True)
                pj_rel[pb] = w_stash
                w_tanh = C(lambda c: c.activation(out=a_tn[:, :], in_=a_acc[:, :], func=AF.Tanh, scale=0.5),
                           waits=[w_acc, silu_rel[0]], inc=True)
                w_X = V(lambda v, it=it, xb=xb: v.scalar_tensor_tensor(out=a_X[xb][:, it * TT:(it + 1) * TT],
                                                                       in0=a_tn[:, :], scalar=1.0, in1=a_acc[:, :],
                                                                       op0=ALU.add, op1=ALU.mult),
                        waits=[w_tanh] + (aX_rel[xb] if it == 0 else []), inc=True)
                acc_rel[0] = w_X
                silu_rel[0] = w_X
            a_ln_w = [("sTm", S.cur("sTm"))] + a_ln_next

            w_dt = S.mm(lambda: pdt[0:32, :],
                        [((lambda it=it: kdtw[:, it * 32:(it + 1) * 32]),
                          (lambda it=it, xb=xb: a_X[xb][:, it * TT:(it + 1) * TT])) for it in range(NIT)],
                        waits=[w_X, pdt_w[0], w_dtwh[h][0] if ti == 0 else None])
            V(lambda v: v.tensor_copy(out=d_zb[:, :], in_=pdt[0:32, :]),
              waits=[w_dt, dtb_rel[0]])
            V(lambda v, h=h: v.tensor_scalar(out=d_zb[:, :], in0=d_zb[:, :], scalar1=k_dtb[:, h:h + 1], scalar2=None, op0=ALU.add))
            w_zb = V(lambda v: v.tensor_copy(out=d_t[:, :], in_=d_zb[:, :]), inc=True)
            pdt_w[0] = w_zb
            C(lambda c: c.activation(out=d_u[:, :], in_=d_t[:, :], func=AF.Abs), waits=[w_zb])
            w_u = C(lambda c: c.activation(out=d_u[:, :], in_=d_u[:, :], func=AF.Exp, scale=-1.0), inc=True)
            V(lambda v: v.tensor_scalar(out=d_y[:, :], in0=d_u[:, :], scalar1=-0.25, scalar2=1.0 / 3.0, op0=ALU.mult, op1=ALU.add), waits=[w_u])
            V(lambda v: v.tensor_tensor(out=d_y[:, :], in0=d_y[:, :], in1=d_u[:, :], op=ALU.mult))
            V(lambda v: v.tensor_scalar(out=d_y[:, :], in0=d_y[:, :], scalar1=-0.5, scalar2=None, op0=ALU.add))
            V(lambda v: v.tensor_tensor(out=d_y[:, :], in0=d_y[:, :], in1=d_u[:, :], op=ALU.mult))
            V(lambda v: v.tensor_scalar(out=d_y[:, :], in0=d_y[:, :], scalar1=1.0, scalar2=None, op0=ALU.add))
            w_ny = V(lambda v: v.tensor_tensor(out=d_y[:, :], in0=d_y[:, :], in1=d_u[:, :], op=ALU.mult), inc=True)
            V(lambda v: v.tensor_scalar(out=d_u[:, :], in0=d_u[:, :], scalar1=1.0, scalar2=None, op0=ALU.add))
            for _ in range(2):
                w_e = C(lambda c: c.activation(out=d_t[:, :], in_=d_y[:, :], func=AF.Exp, scale=-1.0), waits=[w_ny], inc=True)
                V(lambda v: v.tensor_tensor(out=d_t[:, :], in0=d_t[:, :], in1=d_u[:, :], op=ALU.mult), waits=[w_e])
                w_ny = V(lambda v: v.scalar_tensor_tensor(out=d_y[:, :], in0=d_t[:, :], scalar=-1.0, in1=d_y[:, :], op0=ALU.add, op1=ALU.add), inc=True)
            V(lambda v: v.tensor_scalar(out=d_t[:, :], in0=d_zb[:, :], scalar1=0.0, scalar2=None, op0=ALU.max))
            V(lambda v: v.tensor_tensor(out=d_y[:, :], in0=d_t[:, :], in1=d_y[:, :], op=ALU.add))
            w_dtbf = V(lambda v, xb=xb: v.tensor_copy(out=d_bf[xb][:, :], in_=d_y[:, :]), waits=dbf_rel[xb], inc=True)
            dtb_rel[0] = w_dtbf
            st["w_dtbf"] = w_dtbf
            st["t0"] = t0
            st["xb"] = xb
            TST[ti] = st

        def phase2(ti):
            st = TST[ti]
            xb = st["xb"]
            t0 = st["t0"]
            w_dtbf = st["w_dtbf"]
            w_vm_last = None
            w_y0 = None
            for kk in range(NIT):
                k_g = h * 16 + kk
                sb = kk % 2
                pdecb = [pdec, pst][sb]
                pdrvb = [pdrv, psq][sb]
                w_dm = S.mm(lambda pdecb=pdecb: pdecb[:, :],
                            [((lambda kk=kk: mdec[:, kk * 128:(kk + 1) * 128]), (lambda xb=xb: d_bf[xb][:, :]))],
                            waits=[w_dtbf, pdec_w[sb], stats_w[0] if (sb == 1 and kk == 1) else None,
                                   w_mdec[h] if (ti == 0 and kk == 0) else None])
                w_vm = S.mm(lambda pdrvb=pdrvb: pdrvb[:, :],
                            [((lambda kk=kk: kbp[:, kk * 128:(kk + 1) * 128]), (lambda xb=xb: d_bf[xb][:, :]))],
                            waits=[pdrv_w[sb], bc_w[1] if (sb == 1 and kk == 1) else None,
                                   w_dtwh[h][1] if (ti == 0 and kk == 0) else None])
                w_vm_last = w_vm
                w_dec = C(lambda c, sb=sb, pdecb=pdecb: c.activation(out=f_dec[sb][:, :], in_=pdecb[:, :], func=AF.Exp),
                          waits=[w_dm, dec_rel[sb]], inc=True)
                pdec_w[sb] = w_dec
                w_drv = V(lambda v, kk=kk, xb=xb, sb=sb, pdrvb=pdrvb: v.tensor_tensor(out=f_drv[sb][:, :], in0=pdrvb[:, :], in1=a_X[xb][:, kk * TT:(kk + 1) * TT], op=ALU.mult),
                          waits=[w_vm], inc=True)
                pdrv_w[sb] = w_drv
                V(lambda v, k_g=k_g, sb=sb: v.tensor_tensor_scan(out=f_st[sb][:, :], data0=f_dec[sb][:, :], data1=f_drv[sb][:, :],
                                                                 initial=p_fcar[:, k_g:k_g + 1], op0=ALU.mult, op1=ALU.add),
                  waits=[w_dec, st_rel[sb]])
                V(lambda v, k_g=k_g, sb=sb: v.tensor_tensor_scan(out=f_ca[sb][:, :], data0=f_dec[sb][:, :], data1=zerot[:, :],
                                                                 initial=p_dcar[:, k_g:k_g + 1], op0=ALU.mult, op1=ALU.add),
                  waits=[ca_rel[sb]])
                V(lambda v, k_g=k_g, sb=sb: v.tensor_copy(out=p_fcar[:, k_g:k_g + 1], in_=f_st[sb][:, TT - 1:TT]))
                w_car = V(lambda v, k_g=k_g, sb=sb: v.tensor_copy(out=p_dcar[:, k_g:k_g + 1], in_=f_ca[sb][:, TT - 1:TT]), inc=True)
                dec_rel[sb] = w_car
                ob = kk % 2
                w_t1 = C(lambda c, k_g=k_g, sb=sb: c.activation(out=a_t1[:, :], in_=f_st[sb][:, :], func=AF.Copy, scale=k_cp2[:, k_g:k_g + 1]),
                         waits=[w_car, w_cp2 if (ti == 0 and kk == 0) else None, t1_rel[0]], inc=True)
                st_rel[sb] = w_t1
                w_ag = C(lambda c, k_g=k_g, ob=ob, sb=sb: c.activation(out=a_ag[ob][:, :], in_=f_ca[sb][:, :], func=AF.Copy, scale=k_cp2[:, k_g:k_g + 1]),
                         waits=(st_w[ob] or []) + [w_dp2 if (ti == 0 and kk == 0) else None], inc=True)
                ca_rel[sb] = w_ag
                w_y0 = V(lambda v, k_g=k_g, kk=kk, ob=ob, xb=xb: v.scalar_tensor_tensor(out=a_y0[ob][:, :], in0=a_X[xb][:, kk * TT:(kk + 1) * TT],
                                                                                        scalar=k_dp2[:, k_g:k_g + 1], in1=a_t1[:, :],
                                                                                        op0=ALU.mult, op1=ALU.add),
                         waits=[w_t1] + (st_w[ob] or []), inc=True)
                t1_rel[0] = w_y0
                st1 = S.store(sp_y0.rearrange("p (a b) -> p a b", a=32)[:, k_g, t0:t0 + TT], a_y0[ob][:, :], waits=[w_y0], slot=f"sy{ob}")
                st2 = S.store(sp_ag.rearrange("p (a b) -> p a b", a=32)[:, k_g, t0:t0 + TT], a_ag[ob][:, :], waits=[w_ag], slot=f"sa{ob}")
                st_w[ob] = [st1, st2]
            aX_rel[xb][:] = [w_vm_last, w_y0]
            dbf_rel[xb][:] = [w_vm_last]
            p2rel[0] = pdec_w[1]
            p2rel[1] = pdrv_w[1]

        phase1(0)
        for ti in range(NTILE):
            if ti + 1 < NTILE:
                phase1(ti + 1)
            phase2(ti)

    pj_rel = [None, None]
    acc_rel = [None]
    silu_rel = [None]
    t1_rel = [None]
    aX_rel = [[], []]
    dtb_rel = [None]
    dbf_rel = [[], []]
    dec_rel = [None, None]
    drv_rel = [None]
    st_rel = [None, None]
    ca_rel = [None, None]
    a_ln_w = []
    a_x_w = [w_memset]

    if stage_limit >= 1:
        stage_A(0)
        a_ln_w = [("sTm", S.cur("sTm"))]
        S.barrier()
    if stage_limit >= 2:
        stage_A(1)
        S.barrier()

    if stage_limit >= 3:
        w_cc1 = V(lambda v: v.tensor_copy(out=cc_sb[:, 0:32], in_=p_fcar[:, :]))
        w_cc = V(lambda v: v.tensor_copy(out=cc_sb[:, 32:64], in_=p_dcar[:, :]), inc=True)
        stc = S.store(cc_in, cc_sb[:, :], waits=[w_cc], slot="scc")
        S.wait("gpsimd", stc)
        S.emit("gpsimd", lambda g: g.collective_compute(
            "AllGather", ALU.bypass, replica_groups=[list(range(NCORES))],
            ins=[cc_in.opt()], outs=[cc_out.opt()]).then_inc(sem["sCC"]))
        S.inc("sCC", 1)
        S.emit("gpsimd", lambda g: g.wait_ge(sem["sCC"], 1))
        gl = None
        for r in range(NCORES):
            gl = S.load(cc_g[:, 64 * r:64 * r + 64], cc_out[128 * r:128 * (r + 1), :], slot="ccg")
        for r in range(NCORES):
            V(lambda v, r=r: v.tensor_tensor(out=si_t[:, :], in0=cc_g[:, 64 * r + 32:64 * r + 64], in1=p_sin[:, :], op=ALU.mult),
              waits=[gl, w_memset] if r == 0 else (), chain=True)
            V(lambda v, r=r: v.tensor_tensor(out=si_t[:, :], in0=si_t[:, :], in1=cc_g[:, 64 * r:64 * r + 32], op=ALU.add), chain=True)
            V(lambda v: v.tensor_tensor(out=si_t2[:, :], in0=si_t[:, :], in1=p_sin[:, :], op=ALU.subtract), chain=True)
            V(lambda v, r=r: v.scalar_tensor_tensor(out=p_sin[:, :], in0=si_t2[:, :], scalar=k_mask[:, r:r + 1], in1=p_sin[:, :], op0=ALU.mult, op1=ALU.add), chain=True)
        S.barrier()
    g1.close()

    g2 = ExitStack()
    wB = g2.enter_context(nc.sbuf_tensor("wB", [128, TILEW], BF16))
    b_ln = [g2.enter_context(nc.sbuf_tensor(f"b_ln{i}", [128, 16 * TT], BF16)) for i in range(2)]
    b_y0 = [g2.enter_context(nc.sbuf_tensor(f"b_y0{i}", [128, TT], BF16)) for i in range(4)]
    b_ag = [g2.enter_context(nc.sbuf_tensor(f"b_ag{i}", [128, TT], BF16)) for i in range(4)]
    b_t = g2.enter_context(nc.sbuf_tensor("b_t", [128, TT], F32))
    b_y = g2.enter_context(nc.sbuf_tensor("b_y", [128, TT], F32))
    b_cb = [g2.enter_context(nc.sbuf_tensor(f"b_cb{i}", [128, TT], BF16)) for i in range(4)]
    pg = [g2.enter_context(nc.psum_tensor(f"pg{i}", [128, TT], F32)) for i in range(2)]

    def stage_B(h):
        wtc = []
        for ckk in range(4):
            wtc.append(S.load(wB[:, ckk * 8192:(ckk + 1) * 8192],
                              gated[:, h * TILEW + ckk * 8192:h * TILEW + (ckk + 1) * 8192],
                              slot="wB"))
        pg_w = [None, None]
        cb_w = [None, None, None, None]
        ln_w = [None, None]
        bt_w = None
        for ti in range(NTILE):
            t0 = ti * TT
            lb = ti % 2
            lnw = S.load(b_ln[lb][:, :].rearrange("p (a b) -> p a b", a=16),
                         sp_lnx.rearrange("p (a b) -> p a b", a=16)[:, :, 3 + t0:3 + t0 + TT],
                         waits=[ln_w[lb]], slot=f"bln{lb}")
            for kk in range(NIT):
                k_g = h * 16 + kk
                ob = kk % 4
                ly = S.load(b_y0[ob][:, :], sp_y0.rearrange("p (a b) -> p a b", a=32)[:, k_g, t0:t0 + TT],
                            waits=[cb_w[ob]], slot=f"by{ob}")
                la = S.load(b_ag[ob][:, :], sp_ag.rearrange("p (a b) -> p a b", a=32)[:, k_g, t0:t0 + TT],
                            waits=[cb_w[ob]], slot=f"ba{ob}")
                w_mm = S.mm(lambda ob=ob: pg[ob % 2][:, :],
                            [((lambda dt=dt, kk=kk: wB[:, (kk * 16 + dt) * 128:(kk * 16 + dt) * 128 + 128]),
                              (lambda dt=dt, lb=lb: b_ln[lb][:, dt * TT:(dt + 1) * TT])) for dt in range(ND)],
                            waits=[lnw, wtc[kk // 4] if ti == 0 else None, pg_w[ob % 2]])
                w_tanh = C(lambda c, ob=ob: c.activation(out=b_t[:, :], in_=pg[ob % 2][:, :], func=AF.Tanh, scale=0.5),
                           waits=[w_mm, bt_w], inc=True)
                pg_w[ob % 2] = w_tanh
                w_y = V(lambda v, k_g=k_g, ob=ob: v.scalar_tensor_tensor(out=b_y[:, :], in0=b_ag[ob][:, :],
                                                                         scalar=p_sin[:, k_g:k_g + 1], in1=b_y0[ob][:, :],
                                                                         op0=ALU.mult, op1=ALU.add),
                        waits=[la, ly], inc=True)
                w_cb = V(lambda v, ob=ob: v.scalar_tensor_tensor(out=b_cb[ob][:, :], in0=b_t[:, :], scalar=1.0,
                                                                 in1=b_y[:, :], op0=ALU.add, op1=ALU.mult),
                         waits=[w_tanh, cb_w[ob]], inc=True)
                bt_w = w_cb
                st = S.store(sp_comb.rearrange("p (a b) -> p a b", a=32)[:, k_g, t0:t0 + TT], b_cb[ob][:, :], waits=[w_cb], slot=f"scb{ob}")
                cb_w[ob] = st
            ln_w[lb] = ("sTm", S.cur("sTm"))

    if stage_limit >= 4:
        stage_B(0)
        S.barrier()
        stage_B(1)
        S.barrier()
    g2.close()

    g3 = ExitStack()
    wC = g3.enter_context(nc.sbuf_tensor("wC", [128, TILEW], BF16))
    c_cb = [g3.enter_context(nc.sbuf_tensor(f"c_cb{i}", [128, 32 * TC], BF16)) for i in range(2)]
    c_mo = g3.enter_context(nc.sbuf_tensor("c_mo", [128, 1024], BF16))
    c_mo1 = [g3.enter_context(nc.sbuf_tensor(f"c_mo1{i}", [128, 1024], BF16)) for i in range(2)]
    c_xt = [g3.enter_context(nc.sbuf_tensor(f"c_xt{i}", [128, 2048], F32)) for i in range(2)]
    c_x1 = g3.enter_context(nc.sbuf_tensor("c_x1", [128, 2048], F32))
    c_ln1 = g3.enter_context(nc.sbuf_tensor("c_ln1", [128, 2048], BF16))
    c_sq = g3.enter_context(nc.sbuf_tensor("c_sq", [128, 1024], F32))
    c_st = g3.enter_context(nc.sbuf_tensor("c_st", [128, 2], F32))
    c_stp = [g3.enter_context(nc.sbuf_tensor(f"c_stp{i}", [128, 2], F32)) for i in range(2)]
    c_m = g3.enter_context(nc.sbuf_tensor("c_m", [128, 1], F32))
    c_v = g3.enter_context(nc.sbuf_tensor("c_v", [128, 1], F32))
    c_r = g3.enter_context(nc.sbuf_tensor("c_r", [128, 1], F32))
    c_w1 = g3.enter_context(nc.sbuf_tensor("c_w1", [128, 1], F32))
    c_w2 = g3.enter_context(nc.sbuf_tensor("c_w2", [128, 1], F32))
    c_tt = g3.enter_context(nc.sbuf_tensor("c_tt", [128, 1024], F32))
    pc = [g3.enter_context(nc.psum_tensor(f"pc{i}", [128, 1024], F32)) for i in range(2)]

    NTC = CHUNK // TC

    def stage_C(dh):
        wtc = []
        for ckk in range(2):
            wtc.append(S.load(wC[:, ckk * 16384:(ckk + 1) * 16384],
                              outPd[:, dh * TILEW + ckk * 16384:dh * TILEW + (ckk + 1) * 16384],
                              slot="wC"))
        pc_w = [[], []]
        cb_ld = [None, None]
        mo_st = None
        st_st = None
        sq_w = None
        x1_w = None
        ln1_w = None
        xt_w = [None, None]
        mo1_w = [None, None]
        stp_w = [None, None]
        for ti in range(NTC):
            t0 = ti * TC
            lb = ti % 2
            lc = S.load(c_cb[lb][:, :].rearrange("p (a b) -> p a b", a=32),
                        sp_comb.rearrange("p (a b) -> p a b", a=32)[:, :, t0:t0 + TC],
                        waits=[cb_ld[lb]], slot=f"cb{lb}")
            for ts in range(2):
                if c_parts < 2:
                    continue
                tb = ti * 2 + ts
                pb = ts
                pairs_a = [((lambda ct=ct, ts=ts, lb=lb: c_cb[lb][:, ct * TC + ts * 128:ct * TC + ts * 128 + 128]),
                            (lambda ct=ct: wC[:, ct * 512:ct * 512 + 512])) for ct in range(32)]
                pairs_b = [((lambda ct=ct, ts=ts, lb=lb: c_cb[lb][:, ct * TC + ts * 128:ct * TC + ts * 128 + 128]),
                            (lambda ct=ct: wC[:, 16384 + ct * 512:16384 + ct * 512 + 512])) for ct in range(32)]
                w_a = S.mm(lambda pb=pb: pc[pb][:, 0:512], pairs_a,
                           waits=[lc, wtc[0] if tb == 0 else None] + (pc_w[pb] or []))
                w_b = S.mm(lambda pb=pb: pc[pb][:, 512:1024], pairs_b,
                           waits=[wtc[1] if tb == 0 else None])
                if c_stats:
                    w_sq = C(lambda c, pb=pb: c.activation(out=c_sq[:, :], in_=pc[pb][:, :], func=AF.Square),
                             waits=[w_a, w_b, sq_w], inc=True)
                    w_r1 = V(lambda v, pb=pb: v.tensor_reduce(out=c_st[:, 0:1], in_=pc[pb][:, :], axis=AX.X, op=ALU.add),
                             waits=[w_sq, st_st], chain=True)
                    w_red = V(lambda v: v.tensor_reduce(out=c_st[:, 1:2], in_=c_sq[:, :], axis=AX.X, op=ALU.add),
                              inc=True, chain=True)
                    sq_w = w_red
                else:
                    w_sq = None
                    w_r1 = [w_a, w_b]
                    w_red = ("sVe", None)
                if c_parts < 3:
                    pc_w[pb] = [w_a, w_b]
                    continue
                if dh == 0:
                    w_ev = C(lambda c, pb=pb: c.activation(out=c_mo[:, :], in_=pc[pb][:, :], func=AF.Copy),
                             waits=([mo_st, w_r1] if c_stats else [mo_st] + w_r1), inc=True)
                    pc_w[pb] = [w_ev]
                    mo_st = S.store(sp_mo1[tb * 128:(tb + 1) * 128, :], c_mo[:, :], waits=[w_ev], slot="cmo")
                    if c_stats:
                        st_st = S.store(sp_st2[tb * 128:(tb + 1) * 128, :], c_st[:, :], waits=[w_red], slot="cst")
                else:
                    lm = S.load(c_mo1[pb][:, :], sp_mo1[tb * 128:(tb + 1) * 128, :], waits=[mo1_w[pb]], slot=f"cm{pb}")
                    lp = S.load(c_stp[pb][:, :], sp_st2[tb * 128:(tb + 1) * 128, :], waits=[stp_w[pb]], slot=f"cp{pb}")
                    lx = S.load(c_xt[pb][:, :], xtokd[tb * 128:(tb + 1) * 128, :], waits=[xt_w[pb]], slot=f"cx{pb}")
                    w_st = V(lambda v, pb=pb: v.tensor_tensor(out=c_st[:, :], in0=c_st[:, :], in1=c_stp[pb][:, :], op=ALU.add),
                             waits=[lp, w_red], inc=True, chain=True)
                    stp_w[pb] = w_st
                    st_st = w_st
                    V(lambda v: v.tensor_scalar(out=c_m[:, :], in0=c_st[:, 0:1], scalar1=1.0 / D_, scalar2=None, op0=ALU.mult), chain=True)
                    V(lambda v: v.tensor_scalar(out=c_v[:, :], in0=c_st[:, 1:2], scalar1=1.0 / D_, scalar2=None, op0=ALU.mult), chain=True)
                    V(lambda v: v.tensor_tensor(out=c_w1[:, :], in0=c_m[:, :], in1=c_m[:, :], op=ALU.mult), chain=True)
                    V(lambda v: v.tensor_tensor(out=c_v[:, :], in0=c_v[:, :], in1=c_w1[:, :], op=ALU.subtract), chain=True)
                    V(lambda v: v.tensor_scalar(out=c_v[:, :], in0=c_v[:, :], scalar1=EPS, scalar2=None, op0=ALU.add), chain=True)
                    S.rsqrt(c_r[:, 0:1], c_w1[:, 0:1], c_v[:, 0:1])
                    V(lambda v, pb=pb: v.tensor_scalar(out=c_tt[:, :], in0=c_mo1[pb][:, :], scalar1=c_m[:, 0:1], scalar2=c_r[:, 0:1], op0=ALU.subtract, op1=ALU.mult),
                      waits=[lm, x1_w, ln1_w])
                    w_p1 = V(lambda v, pb=pb: v.tensor_tensor(out=c_x1[:, 0:1024], in0=c_tt[:, :], in1=c_xt[pb][:, 0:1024], op=ALU.add),
                             waits=[lx], inc=True)
                    mo1_w[pb] = w_p1
                    V(lambda v, pb=pb: v.tensor_scalar(out=c_tt[:, :], in0=pc[pb][:, :], scalar1=c_m[:, 0:1], scalar2=c_r[:, 0:1], op0=ALU.subtract, op1=ALU.mult))
                    w_x1 = V(lambda v, pb=pb: v.tensor_tensor(out=c_x1[:, 1024:2048], in0=c_tt[:, :], in1=c_xt[pb][:, 1024:2048], op=ALU.add), inc=True)
                    pc_w[pb] = [w_x1]
                    xt_w[pb] = w_x1
                    w_sq3 = C(lambda c: c.activation(out=c_sq[:, :], in_=c_x1[:, 0:1024], func=AF.Square),
                              waits=[w_x1, sq_w], inc=True)
                    w_sq3b = C(lambda c: c.activation(out=c_tt[:, :], in_=c_x1[:, 1024:2048], func=AF.Square), inc=True)
                    V(lambda v: v.tensor_reduce(out=c_st[:, 0:1], in_=c_x1[:, :], axis=AX.X, op=ALU.add),
                      waits=[w_x1, w_st], chain=True)
                    V(lambda v: v.tensor_reduce(out=c_w1[:, :], in_=c_sq[:, :], axis=AX.X, op=ALU.add),
                      waits=[w_sq3], chain=True)
                    w_rq = V(lambda v: v.tensor_reduce(out=c_w2[:, :], in_=c_tt[:, :], axis=AX.X, op=ALU.add),
                             waits=[w_sq3b], inc=True, chain=True)
                    sq_w = w_rq
                    V(lambda v: v.tensor_tensor(out=c_w1[:, :], in0=c_w1[:, :], in1=c_w2[:, :], op=ALU.add), chain=True)
                    V(lambda v: v.tensor_scalar(out=c_m[:, :], in0=c_st[:, 0:1], scalar1=1.0 / D_, scalar2=None, op0=ALU.mult), chain=True)
                    V(lambda v: v.tensor_scalar(out=c_v[:, :], in0=c_w1[:, :], scalar1=1.0 / D_, scalar2=None, op0=ALU.mult), chain=True)
                    V(lambda v: v.tensor_tensor(out=c_w2[:, :], in0=c_m[:, :], in1=c_m[:, :], op=ALU.mult), chain=True)
                    V(lambda v: v.tensor_tensor(out=c_v[:, :], in0=c_v[:, :], in1=c_w2[:, :], op=ALU.subtract), chain=True)
                    V(lambda v: v.tensor_scalar(out=c_v[:, :], in0=c_v[:, :], scalar1=EPS, scalar2=None, op0=ALU.add), chain=True)
                    S.rsqrt(c_r[:, 0:1], c_w1[:, 0:1], c_v[:, 0:1])
                    w_ln1 = V(lambda v: v.tensor_scalar(out=c_ln1[:, :], in0=c_x1[:, :], scalar1=c_m[:, 0:1], scalar2=c_r[:, 0:1], op0=ALU.subtract, op1=ALU.mult),
                              waits=[ln1_w], inc=True)
                    st1 = S.store(sp_x1[tb * 128:(tb + 1) * 128, :], c_x1[:, :], waits=[w_ln1], slot="sx1")
                    st2 = S.store(sp_lnx1[tb * 128:(tb + 1) * 128, :], c_ln1[:, :], waits=[w_ln1], slot="sln1")
                    x1_w = st1
                    ln1_w = st2
                    st_st = w_st
            cb_ld[lb] = ("sTm", S.cur("sTm"))

    if stage_limit >= 5:
        stage_C(0)
        S.barrier()
    if stage_limit >= 5.5:
        stage_C(1)
        S.barrier()
    g3.close()

    g4 = ExitStack()
    dres = g4.enter_context(nc.sbuf_tensor("dres", [128, 65536], BF16))
    d_w1 = [g4.enter_context(nc.sbuf_tensor(f"d_w1{i}", [128, 2048], BF16)) for i in range(2)]
    d_w3 = [g4.enter_context(nc.sbuf_tensor(f"d_w3{i}", [128, 2048], BF16)) for i in range(2)]
    d_h128 = g4.enter_context(nc.sbuf_tensor("d_h128", [128, 128], BF16))
    d_tn = g4.enter_context(nc.sbuf_tensor("d_tn", [128, TT], F32))
    d_u_ = g4.enter_context(nc.sbuf_tensor("d_u_", [128, TT], F32))
    d_hid = [g4.enter_context(nc.sbuf_tensor(f"d_hid{i}", [128, TT], BF16)) for i in range(2)]
    d_out = [g4.enter_context(nc.sbuf_tensor(f"d_out{i}", [128, TT], BF16)) for i in range(2)]
    pgq = [g4.enter_context(nc.psum_tensor(f"pgq{i}", [128, TT], F32)) for i in range(2)]
    plq = [g4.enter_context(nc.psum_tensor(f"plq{i}", [128, TT], F32)) for i in range(2)]
    phh = [g4.enter_context(nc.psum_tensor(f"phh{i}", [128, TT], F32)) for i in range(2)]

    if stage_limit >= 6:
        lh = S.load(d_h128[:, :], h128d, slot="h128")
        tl = None
        for ti in range(NTILE):
            for dt in range(ND):
                tl = S.loadT(dres[:, dt * CHUNK + ti * TT:dt * CHUNK + (ti + 1) * TT],
                             sp_lnx1[ti * TT:(ti + 1) * TT, dt * 128:(dt + 1) * 128], slot="dres")
        lnx1_ready = tl

        wd_ld = [None, None]
        pq_w = [None, None]
        pl_w = [None, None]
        ph_w = [None, None]
        do_w = [None, None]
        hid_w = [None, None]
        tn_w = None
        pend = [None]

        def emit_h128(ch, ti, hb, w_hid):
            pb2 = (ch * NTILE + ti) % 2
            w_h = S.mm(lambda pb2=pb2: phh[pb2][:, :],
                       [((lambda: d_h128[:, :]), (lambda hb=hb: d_hid[hb][:, :]))],
                       waits=[w_hid, lh if (ch == 0 and ti == 0) else None, ph_w[pb2]])
            hid_w[hb] = w_h
            ob = pb2
            w_ev = C(lambda c, pb2=pb2, ob=ob: c.activation(out=d_out[ob][:, :], in_=phh[pb2][:, :], func=AF.Copy),
                     waits=[w_h, do_w[ob]], inc=True)
            ph_w[pb2] = w_ev
            st = S.store(sp_hid.rearrange("p (a b) -> p a b", a=64)[:, ch, ti * TT:(ti + 1) * TT], d_out[ob][:, :], waits=[w_ev], slot=f"sd{ob}")
            do_w[ob] = st

        for ch in range(64):
            wb = ch % 2
            l1 = S.load(d_w1[wb][:, :], w1Td[:, ch * 2048:(ch + 1) * 2048], waits=[wd_ld[wb]], slot=f"dw1{wb}")
            l3 = S.load(d_w3[wb][:, :], w3Td[:, ch * 2048:(ch + 1) * 2048], waits=[wd_ld[wb]], slot=f"dw3{wb}")
            for ti in range(NTILE):
                pb = ti % 2
                w_g = S.mm(lambda pb=pb: pgq[pb][:, :],
                           [((lambda dt=dt, wb=wb: d_w1[wb][:, dt * 128:(dt + 1) * 128]),
                             (lambda dt=dt, ti=ti: dres[:, dt * CHUNK + ti * TT:dt * CHUNK + (ti + 1) * TT])) for dt in range(ND)],
                           waits=[l1, lnx1_ready if (ch == 0 and ti == 0) else None, pq_w[pb]])
                w_l = S.mm(lambda pb=pb: plq[pb][:, :],
                           [((lambda dt=dt, wb=wb: d_w3[wb][:, dt * 128:(dt + 1) * 128]),
                             (lambda dt=dt, ti=ti: dres[:, dt * CHUNK + ti * TT:dt * CHUNK + (ti + 1) * TT])) for dt in range(ND)],
                           waits=[l3, pl_w[pb]])
                w_tn = C(lambda c, pb=pb: c.activation(out=d_tn[:, :], in_=pgq[pb][:, :], func=AF.Tanh, scale=0.5),
                         waits=[w_g, tn_w], inc=True)
                w_u = V(lambda v, pb=pb: v.scalar_tensor_tensor(out=d_u_[:, :], in0=d_tn[:, :], scalar=1.0, in1=pgq[pb][:, :], op0=ALU.add, op1=ALU.mult),
                        waits=[w_tn], inc=True)
                pq_w[pb] = w_u
                tn_w = w_u
                w_hid = V(lambda v, pb=pb, hb=pb: v.scalar_tensor_tensor(out=d_hid[hb][:, :], in0=d_u_[:, :], scalar=0.5, in1=plq[pb][:, :], op0=ALU.mult, op1=ALU.mult),
                          waits=[w_l, hid_w[pb]], inc=True)
                pl_w[pb] = w_hid
                if pend[0] is not None:
                    emit_h128(*pend[0])
                pend[0] = (ch, ti, pb, w_hid)
            wd_ld[wb] = ("sTm", S.cur("sTm"))
        if pend[0] is not None:
            emit_h128(*pend[0])
        S.barrier()
    g4.close()

    g5 = ExitStack()
    eres = g5.enter_context(nc.sbuf_tensor("eres", [128, 65536], BF16))
    e_hid = [g5.enter_context(nc.sbuf_tensor(f"e_hid{i}", [128, 8192], BF16)) for i in range(2)]
    e_sq = g5.enter_context(nc.sbuf_tensor("e_sq", [128, 1024], F32))
    e_st = g5.enter_context(nc.sbuf_tensor("e_st", [128, 2], F32))
    e_stp = g5.enter_context(nc.sbuf_tensor("e_stp", [128, 2], F32))
    e_f1 = g5.enter_context(nc.sbuf_tensor("e_f1", [128, 1024], BF16))
    e_x1 = g5.enter_context(nc.sbuf_tensor("e_x1", [128, 2048], F32))
    e_o = g5.enter_context(nc.sbuf_tensor("e_o", [128, 2048], F32))
    e_tt = g5.enter_context(nc.sbuf_tensor("e_tt", [128, 1024], F32))
    e_m = g5.enter_context(nc.sbuf_tensor("e_m", [128, 1], F32))
    e_v = g5.enter_context(nc.sbuf_tensor("e_v", [128, 1], F32))
    e_r = g5.enter_context(nc.sbuf_tensor("e_r", [128, 1], F32))
    e_w1 = g5.enter_context(nc.sbuf_tensor("e_w1", [128, 1], F32))
    e_mo = g5.enter_context(nc.sbuf_tensor("e_mo", [128, 1024], BF16))
    pe = [g5.enter_context(nc.psum_tensor(f"pe{i}", [128, 1024], F32)) for i in range(2)]

    NB = CHUNK // 128

    def stage_E(dh):
        wls = None
        for o in range(64):
            src = w2f[o:o + 1, :].rearrange("a (h d) -> (a h) d", h=128)[:, dh * 1024:(dh + 1) * 1024]
            wls = S.load(eres[:, o * 1024:(o + 1) * 1024], src, slot="eres")
        pe_w = [[], []]
        hid_ld = [None, None]
        sq_w = None
        st_w = None
        f1_w = None
        o_w = None
        stp_w = None
        for tb in range(NB):
            hb = tb % 2
            lhd = S.load(e_hid[hb][:, :].rearrange("p (a b) -> p a b", a=64),
                         sp_hid.rearrange("p (a b) -> p a b", a=64)[:, :, tb * 128:(tb + 1) * 128],
                         waits=[hid_ld[hb]], slot=f"eh{hb}")
            pb = tb % 2
            pairs_a = [((lambda o=o, hb=hb: e_hid[hb][:, o * 128:(o + 1) * 128]),
                        (lambda o=o: eres[:, o * 1024:o * 1024 + 512])) for o in range(64)]
            pairs_b = [((lambda o=o, hb=hb: e_hid[hb][:, o * 128:(o + 1) * 128]),
                        (lambda o=o: eres[:, o * 1024 + 512:o * 1024 + 1024])) for o in range(64)]
            w_a = S.mm(lambda pb=pb: pe[pb][:, 0:512], pairs_a,
                       waits=[lhd, wls if tb == 0 else None] + (pe_w[pb] or []))
            w_b = S.mm(lambda pb=pb: pe[pb][:, 512:1024], pairs_b, waits=[])
            w_sq = C(lambda c, pb=pb: c.activation(out=e_sq[:, :], in_=pe[pb][:, :], func=AF.Square),
                     waits=[w_a, w_b, sq_w], inc=True)
            w_er1 = V(lambda v, pb=pb: v.tensor_reduce(out=e_st[:, 0:1], in_=pe[pb][:, :], axis=AX.X, op=ALU.add),
                      waits=[w_sq, st_w], chain=True)
            w_red = V(lambda v: v.tensor_reduce(out=e_st[:, 1:2], in_=e_sq[:, :], axis=AX.X, op=ALU.add),
                      inc=True, chain=True)
            sq_w = w_red
            if dh == 0:
                w_ev = C(lambda c, pb=pb: c.activation(out=e_f1[:, :], in_=pe[pb][:, :], func=AF.Copy),
                         waits=[f1_w, w_er1], inc=True)
                pe_w[pb] = [w_ev]
                f1_w = S.store(sp_ffn1[tb * 128:(tb + 1) * 128, :], e_f1[:, :], waits=[w_ev], slot="sf1")
                st_w = S.store(sp_st4[tb * 128:(tb + 1) * 128, :], e_st[:, :], waits=[w_red], slot="est4")
            else:
                lf = S.load(e_mo[:, :], sp_ffn1[tb * 128:(tb + 1) * 128, :], waits=[o_w], slot="emo")
                lp = S.load(e_stp[:, :], sp_st4[tb * 128:(tb + 1) * 128, :], waits=[stp_w], slot="estp")
                lx = S.load(e_x1[:, :], sp_x1[tb * 128:(tb + 1) * 128, :], waits=[o_w], slot="ex1")
                w_st = V(lambda v: v.tensor_tensor(out=e_st[:, :], in0=e_st[:, :], in1=e_stp[:, :], op=ALU.add),
                         waits=[lp, w_red], inc=True, chain=True)
                stp_w = w_st
                st_w = w_st
                V(lambda v: v.tensor_scalar(out=e_m[:, :], in0=e_st[:, 0:1], scalar1=1.0 / D_, scalar2=None, op0=ALU.mult), chain=True)
                V(lambda v: v.tensor_scalar(out=e_v[:, :], in0=e_st[:, 1:2], scalar1=1.0 / D_, scalar2=None, op0=ALU.mult), chain=True)
                V(lambda v: v.tensor_tensor(out=e_w1[:, :], in0=e_m[:, :], in1=e_m[:, :], op=ALU.mult), chain=True)
                V(lambda v: v.tensor_tensor(out=e_v[:, :], in0=e_v[:, :], in1=e_w1[:, :], op=ALU.subtract), chain=True)
                V(lambda v: v.tensor_scalar(out=e_v[:, :], in0=e_v[:, :], scalar1=EPS, scalar2=None, op0=ALU.add), chain=True)
                S.rsqrt(e_r[:, 0:1], e_w1[:, 0:1], e_v[:, 0:1])
                V(lambda v: v.tensor_scalar(out=e_tt[:, :], in0=e_mo[:, :], scalar1=e_m[:, 0:1], scalar2=e_r[:, 0:1], op0=ALU.subtract, op1=ALU.mult),
                  waits=[lf])
                V(lambda v: v.tensor_tensor(out=e_o[:, 0:1024], in0=e_tt[:, :], in1=e_x1[:, 0:1024], op=ALU.add),
                  waits=[lx, o_w])
                V(lambda v, pb=pb: v.tensor_scalar(out=e_tt[:, :], in0=pe[pb][:, :], scalar1=e_m[:, 0:1], scalar2=e_r[:, 0:1], op0=ALU.subtract, op1=ALU.mult))
                w_o = V(lambda v: v.tensor_tensor(out=e_o[:, 1024:2048], in0=e_tt[:, :], in1=e_x1[:, 1024:2048], op=ALU.add), inc=True)
                pe_w[pb] = [w_o]
                o_w = S.store(outd[tb * 128:(tb + 1) * 128, :], e_o[:, :], waits=[w_o], slot="out")
            hid_ld[hb] = ("sTm", S.cur("sTm"))

    if stage_limit >= 7:
        stage_E(0)
        S.barrier()
        stage_E(1)

    for sname in [s for s in S.cnt if s.startswith("S_")]:
        S.wait("sync", (sname, S.cur(sname)))
        S.wait("gpsimd", (sname, S.cur(sname)))

    block = es.enter_context(nc.Block())
    block.sync(lambda e: [f(e) for f in S.q["sync"]])
    block.gpsimd(lambda e: [f(e) for f in S.q["gpsimd"]])
    block.vector(lambda e: [f(e) for f in S.q["vector"]])
    block.scalar(lambda e: [f(e) for f in S.q["scalar"]])
    block.tensor(lambda e: [f(e) for f in S.q["tensor"]])
    g5.close()
    es.close()
    return nc



def _stage_inputs(inputs):
    x = np.asarray(inputs["L0_attn_hidden_states"], dtype=np.float32)
    inw = np.asarray(inputs["L0_attn_in_proj_w"], dtype=np.float32)
    gw = np.asarray(inputs["L0_attn_gate_proj_w"], dtype=np.float32)
    ow = np.asarray(inputs["L0_attn_out_proj_w"], dtype=np.float32)
    convw = np.asarray(inputs["L0_attn_conv_w"], dtype=np.float32)
    convb = np.asarray(inputs["L0_attn_conv_b"], dtype=np.float32)
    dtw = np.asarray(inputs["L0_attn_dt_w"], dtype=np.float32)
    dtb = np.asarray(inputs["L0_attn_dt_b"], dtype=np.float32)
    alog = np.asarray(inputs["L0_attn_A_log"], dtype=np.float32)
    bp = np.asarray(inputs["L0_attn_Bp"], dtype=np.float32)
    cp = np.asarray(inputs["L0_attn_Cp"], dtype=np.float32)
    dp = np.asarray(inputs["L0_attn_Dp"], dtype=np.float32)
    w1 = np.asarray(inputs["L0_ffn_W1"], dtype=np.float32)
    w2 = np.asarray(inputs["L0_ffn_W2"], dtype=np.float32)
    w3 = np.asarray(inputs["L0_ffn_W3"], dtype=np.float32)

    def tile_proj(w):
        wT = w.T.reshape(16, 128, 2, 16, 128)
        return np.ascontiguousarray(wT.transpose(1, 2, 3, 0, 4).reshape(128, 65536))

    def tile_ffn(w):
        wT = w.T.reshape(16, 128, 64, 128)
        return np.ascontiguousarray(wT.transpose(1, 2, 0, 3).reshape(128, 131072))

    oT = ow.T.reshape(32, 128, 2, 2, 512)
    outP_h = np.ascontiguousarray(oT.transpose(1, 2, 3, 0, 4).reshape(128, 65536))
    w2a = np.ascontiguousarray(w2.reshape(2048, 64, 128).transpose(1, 2, 0).reshape(64, 262144))

    dtw_h, bp_h, al_h, msk_h = [], [], [], []
    for h in range(2):
        dws = np.zeros((2048, 32), dtype=np.float32)
        bps = np.zeros((32, 2048), dtype=np.float32)
        als = np.zeros((32, 2048), dtype=np.float32)
        mss = np.zeros((32, 2048), dtype=np.float32)
        for hh in range(32):
            hg = 32 * h + hh
            base = (hg * 64) % 2048
            dws[base:base + 64, hh] = dtw[hg]
            bps[hh, base:base + 64] = bp[hg]
            als[hh, base:base + 64] = alog[hg]
            mss[hh, base:base + 64] = 1.0
        dtw_h.append(np.ascontiguousarray(dws.reshape(16, 128, 32).transpose(1, 0, 2).reshape(128, 512)))
        bp_h.append(np.ascontiguousarray(bps))
        al_h.append(np.ascontiguousarray(als))
        msk_h.append(np.ascontiguousarray(mss))

    shared = dict(
        inT=tile_proj(inw), gateT=tile_proj(gw), outP=outP_h,
        w1T=tile_ffn(w1), w3T=tile_ffn(w3), w2arr=w2a,
        convw=np.ascontiguousarray(convw.reshape(32, 128, 4).transpose(1, 0, 2).reshape(128, 128)),
        convb=np.ascontiguousarray(convb.reshape(32, 128).T),
        cp=np.ascontiguousarray(cp.reshape(-1).reshape(32, 128).T),
        dp=np.ascontiguousarray(dp.reshape(-1).reshape(32, 128).T),
        dtb=np.ascontiguousarray(dtb.reshape(2, 32).T),
        h64=np.ascontiguousarray(hadamard(64)),
        h128=np.ascontiguousarray(hadamard(128)),
        dtw0=dtw_h[0], dtw1=dtw_h[1], bp0=bp_h[0], bp1=bp_h[1],
        al0=al_h[0], al1=al_h[1], msk0=msk_h[0], msk1=msk_h[1])

    in_maps = []
    for c in range(NCORES):
        b, j = c // 4, c % 4
        s = j * CHUNK
        xe = np.zeros((4099, D_), dtype=np.float32)
        if j > 0:
            xe[:3] = x[b, s - 3:s]
        xe[3:] = x[b, s:s + CHUNK]
        xT = np.ascontiguousarray(xe.T.reshape(16, 128, 4099).transpose(1, 0, 2).reshape(128, 16 * 4099))
        mask = np.zeros((128, 8), dtype=np.float32)
        for r in range(4 * b, c):
            mask[:, r] = 1.0
        m = dict(shared)
        m["xT"] = xT
        m["x_tok"] = np.ascontiguousarray(x[b, s:s + CHUNK])
        m["chmask"] = mask
        in_maps.append(m)
    return in_maps


def kernel(**inputs):
    in_maps = _stage_inputs(inputs)
    nc = build_program()
    res = run_bass_kernel_spmd(nc, in_maps, core_ids=list(range(NCORES)))
    out = np.empty((B_, L_, D_), dtype=np.float32)
    for c in range(NCORES):
        b, j = c // 4, c % 4
        out[b, j * CHUNK:(j + 1) * CHUNK] = np.asarray(res.results[c]["out"])
    return out


# version 10
# speedup vs baseline: 1.0097x; 1.0097x over previous
import numpy as np
from contextlib import ExitStack

import concourse.bass as bass
import concourse.mybir as mybir
from concourse.bass_utils import run_bass_kernel_spmd

F32 = mybir.dt.float32
BF16 = mybir.dt.bfloat16
I32 = mybir.dt.int32
AF = mybir.ActivationFunctionType
ALU = mybir.AluOpType
AX = mybir.AxisListType

NCORES = 8
B_, L_, D_ = 2, 16384, 2048
CHUNK = 4096
TT = 512
NTILE = CHUNK // TT
ND = 16
NIT = 16
TC = 256
EPS = 1e-5
RS8192 = float(1.0 / np.sqrt(8192.0))
MAGIC = 0x5F3759DF
TILEW = 32768

ENGS = ("sync", "gpsimd", "vector", "scalar", "tensor")
OWNSEM = {"gpsimd": "sLg", "sync": "sSt", "tensor": "sTm", "vector": "sVe",
          "scalar": "sSc"}


def hadamard(n):
    H = np.array([[1.0]], dtype=np.float32)
    while H.shape[0] < n:
        H = np.block([[H, H], [H, -H]]).astype(np.float32)
    return H


class Builder:
    def __init__(self, nc, sem, es):
        self.nc = nc
        self.sem = sem
        self.es = es
        self.q = {e: [] for e in ENGS}
        self.cnt = {}
        self.bar = 0

    def slot_sem(self, name):
        if name not in self.sem:
            self.sem[name] = self.es.enter_context(self.nc.semaphore(name))
        return self.sem[name]

    def emit(self, eng, fn):
        self.q[eng].append(fn)

    def inc(self, s, k=1):
        self.cnt[s] = self.cnt.get(s, 0) + k
        return self.cnt[s]

    def cur(self, s):
        return self.cnt.get(s, 0)

    def wait(self, eng, w):
        if w is None or w[1] is None or w[1] <= 0:
            return
        s, v = w
        self.emit(eng, lambda e, s=s, v=v: e.wait_ge(self.sem[s], v))

    def waits(self, eng, ws):
        seen = {}
        for w in ws:
            if w is None or w[1] is None:
                continue
            seen[w[0]] = max(seen.get(w[0], 0), w[1])
        for s, v in seen.items():
            self.wait(eng, (s, v))

    def load(self, dst, src, waits=(), slot="g"):
        sn = "L_" + slot
        self.slot_sem(sn)
        self.waits("gpsimd", waits)
        t = self.inc(sn, 16)
        self.emit("gpsimd", lambda g, d=dst, s=src, sn=sn: g.dma_start(out=d, in_=s).then_inc(self.sem[sn], 16))
        return (sn, t)

    def store(self, dst, src, waits=(), slot="s"):
        sn = "S_" + slot
        self.slot_sem(sn)
        self.waits("sync", waits)
        t = self.inc(sn, 16)
        self.emit("sync", lambda e, d=dst, s=src, sn=sn: e.dma_start(out=d, in_=s).then_inc(self.sem[sn], 16))
        return (sn, t)

    def loadT(self, dst, src, waits=(), slot="t"):
        sn = "S_" + slot
        self.slot_sem(sn)
        self.waits("sync", waits)
        t = self.inc(sn, 16)
        self.emit("sync", lambda e, d=dst, s=src, sn=sn: e.dma_start(out=d, in_=s, transpose=True).then_inc(self.sem[sn], 16))
        return (sn, t)

    def mm(self, psum_fn, pairs, waits=(), start_first=True, stop_last=True):
        self.waits("tensor", waits)
        t = self.inc("sTm")
        n = len(pairs)
        for j, (lf, rf) in enumerate(pairs):
            first = (j == 0) and start_first
            last = j == n - 1
            def _m(e, lf=lf, rf=rf, first=first, last=last, pf=psum_fn, stop=stop_last):
                i = e.matmul(out=pf(), lhsT=lf(), rhs=rf(), start=first, stop=(last and stop))
                if last:
                    i.then_inc(self.sem["sTm"], 1)
            self.emit("tensor", _m)
        return ("sTm", t)

    def vop(self, fn, waits=(), inc=False, chain=False):
        self.waits("vector", waits)
        t = self.inc("sVe") if (inc or chain) else None
        def _f(v, fn=fn, t=t):
            i = fn(v)
            if t is not None:
                i.then_inc(self.sem["sVe"], 1)
        self.emit("vector", _f)
        if chain:
            self.wait("vector", ("sVe", t))
        return ("sVe", t) if (inc or chain) else None

    def cop(self, fn, waits=(), inc=False, chain=False):
        self.waits("scalar", waits)
        t = self.inc("sSc") if (inc or chain) else None
        def _f(c, fn=fn, t=t):
            i = fn(c)
            if t is not None:
                i.then_inc(self.sem["sSc"], 1)
        self.emit("scalar", _f)
        if chain:
            self.wait("scalar", ("sSc", t))
        return ("sSc", t) if (inc or chain) else None

    def barrier(self):
        self.bar += 1
        tgt = 5 * self.bar
        snap = {s: v for s, v in self.cnt.items()}
        for eng in ENGS:
            if eng == "gpsimd":
                own = [s for s in snap if s.startswith("L_") or s == "sLg"]
            elif eng == "sync":
                own = [s for s in snap if s.startswith("S_") or s == "sSt"]
            else:
                own = [OWNSEM[eng]]
            pairs = [(s, snap[s]) for s in own if snap.get(s, 0) > 0]
            def _f(e, pairs=pairs, tgt=tgt):
                for s, v in pairs:
                    e.wait_ge(self.sem[s], v)
                e.nop().then_inc(self.sem["sBar"], 1)
                e.wait_ge(self.sem["sBar"], tgt)
            self.emit(eng, _f)

    def rsqrt(self, buf, tmp, s_ap, waits=(), inc=False, n_iter=2, chain=True):
        bi = buf.bitcast(I32)
        si = s_ap.bitcast(I32)
        self.vop(lambda v: v.tensor_scalar(out=bi, in0=si, scalar1=1, scalar2=None, op0=ALU.logical_shift_right), waits=waits, chain=chain)
        self.vop(lambda v: v.tensor_scalar(out=bi, in0=bi, scalar1=-1, scalar2=None, op0=ALU.mult), chain=chain)
        self.vop(lambda v: v.tensor_scalar(out=bi, in0=bi, scalar1=MAGIC, scalar2=None, op0=ALU.add), chain=chain)
        w = None
        for it in range(n_iter):
            self.vop(lambda v: v.tensor_tensor(out=tmp, in0=buf, in1=buf, op=ALU.mult), chain=chain)
            self.vop(lambda v: v.tensor_tensor(out=tmp, in0=tmp, in1=s_ap, op=ALU.mult), chain=chain)
            self.vop(lambda v: v.tensor_scalar(out=tmp, in0=tmp, scalar1=-0.5, scalar2=1.5, op0=ALU.mult, op1=ALU.add), chain=chain)
            w = self.vop(lambda v: v.tensor_tensor(out=buf, in0=buf, in1=tmp, op=ALU.mult),
                         inc=(inc and it == n_iter - 1), chain=chain)
        return w


def build_program(debug_outputs=(), stage_limit=99.0, c_stats=True, detect_races=True, c_parts=3):
    nc = bass.Bass("TRN2", target_bir_lowering=False, num_devices=NCORES,
                   detect_race_conditions=detect_races)

    def dram(name, shape, dt, kind="Internal", addr_space=None):
        kw = {}
        if kind != "Internal":
            kw["kind"] = kind
        if name in debug_outputs:
            kw["kind"] = "ExternalOutput"
        if addr_space:
            kw["addr_space"] = addr_space
        return nc.dram_tensor(name, shape, dt, **kw)

    xTd = dram("xT", [128, 16 * 4099], F32, kind="ExternalInput").ap()
    xtokd = dram("x_tok", [CHUNK, D_], F32, kind="ExternalInput").ap()
    ind = dram("inT", [128, 65536], F32, kind="ExternalInput").ap()
    gated = dram("gateT", [128, 65536], F32, kind="ExternalInput").ap()
    outPd = dram("outP", [128, 65536], F32, kind="ExternalInput").ap()
    w1Td = dram("w1T", [128, 131072], F32, kind="ExternalInput").ap()
    w3Td = dram("w3T", [128, 131072], F32, kind="ExternalInput").ap()
    w2ad = dram("w2arr", [64, 262144], F32, kind="ExternalInput").ap()
    dtwd = [dram(f"dtw{h}", [128, 512], F32, kind="ExternalInput").ap() for h in range(2)]
    bpd = [dram(f"bp{h}", [32, 2048], F32, kind="ExternalInput").ap() for h in range(2)]
    ald = [dram(f"al{h}", [32, 2048], F32, kind="ExternalInput").ap() for h in range(2)]
    mskd = [dram(f"msk{h}", [32, 2048], F32, kind="ExternalInput").ap() for h in range(2)]
    convwd = dram("convw", [128, 128], F32, kind="ExternalInput").ap()
    convbd = dram("convb", [128, 32], F32, kind="ExternalInput").ap()
    cpd = dram("cp", [128, 32], F32, kind="ExternalInput").ap()
    dpd = dram("dp", [128, 32], F32, kind="ExternalInput").ap()
    dtbd = dram("dtb", [32, 2], F32, kind="ExternalInput").ap()
    h64d = dram("h64", [64, 64], F32, kind="ExternalInput").ap()
    h128d = dram("h128", [128, 128], F32, kind="ExternalInput").ap()
    maskd = dram("chmask", [128, 8], F32, kind="ExternalInput").ap()
    outd = dram("out", [CHUNK, D_], F32, kind="ExternalOutput").ap()

    sp_lnx = dram("sp_lnx", [128, 16 * 4099], BF16).ap()
    sp_y0 = dram("sp_y0", [128, 32 * CHUNK], BF16).ap()
    sp_ag = dram("sp_ag", [128, 32 * CHUNK], BF16).ap()
    sp_comb = dram("sp_comb", [128, 32 * CHUNK], BF16).ap()
    sp_mo1 = dram("sp_mo1", [CHUNK, 1024], BF16).ap()
    sp_x1 = dram("sp_x1", [CHUNK, D_], F32).ap()
    sp_lnx1 = dram("sp_lnx1", [CHUNK, D_], BF16).ap()
    sp_hid = dram("sp_hid", [128, 64 * CHUNK], BF16).ap()
    sp_ffn1 = dram("sp_ffn1", [CHUNK, 1024], BF16).ap()
    sp_st2 = dram("sp_st2", [CHUNK, 2], F32).ap()
    sp_st4 = dram("sp_st4", [CHUNK, 2], F32).ap()
    w2f = dram("w2f", [64, 262144], BF16).ap()
    cc_in = dram("cc_in", [128, 64], F32).ap()
    cc_out = dram("cc_out", [128 * NCORES, 64], F32, addr_space="Shared").ap()

    es = ExitStack()
    sem = {}
    for name in ("sLg", "sSt", "sCC", "sTm", "sVe", "sSc", "sBar"):
        sem[name] = es.enter_context(nc.semaphore(name))
    S = Builder(nc, sem, es)

    p_sin = es.enter_context(nc.sbuf_tensor("p_sin", [128, 32], F32))
    wAB = es.enter_context(nc.sbuf_tensor("wAB", [128, TILEW], BF16))
    wAB_rel = [None]
    p_fcar = es.enter_context(nc.sbuf_tensor("p_fcar", [128, 32], F32))
    p_dcar = es.enter_context(nc.sbuf_tensor("p_dcar", [128, 32], F32))

    V = lambda fn, **kw: S.vop(fn, **kw)
    C = lambda fn, **kw: S.cop(fn, **kw)

    g1 = ExitStack()
    k_convw = g1.enter_context(nc.sbuf_tensor("k_convw", [128, 128], F32))
    k_convb = g1.enter_context(nc.sbuf_tensor("k_convb", [128, 32], F32))
    k_cp2 = g1.enter_context(nc.sbuf_tensor("k_cp2", [128, 32], F32))
    k_dp2 = g1.enter_context(nc.sbuf_tensor("k_dp2", [128, 32], F32))
    k_dtb = g1.enter_context(nc.sbuf_tensor("k_dtb", [32, 2], F32))
    k_mask = g1.enter_context(nc.sbuf_tensor("k_mask", [128, 8], F32))
    k_ones_bf = g1.enter_context(nc.sbuf_tensor("k_ones_bf", [128, 1], BF16))
    k_ones_row = g1.enter_context(nc.sbuf_tensor("k_ones_row", [1, 128], F32))
    k_dtw = [g1.enter_context(nc.sbuf_tensor(f"k_dtw{h}", [128, 512], BF16)) for h in range(2)]
    k_bp = [g1.enter_context(nc.sbuf_tensor(f"k_bp{h}", [32, 2048], BF16)) for h in range(2)]
    k_mdec = [g1.enter_context(nc.sbuf_tensor(f"k_mdec{h}", [32, 2048], BF16)) for h in range(2)]
    zerot = g1.enter_context(nc.sbuf_tensor("zerot", [128, TT], F32))
    cc_sb = g1.enter_context(nc.sbuf_tensor("cc_sb", [128, 64], F32))
    cc_g = g1.enter_context(nc.sbuf_tensor("cc_g", [128, 512], F32))
    si_t = g1.enter_context(nc.sbuf_tensor("si_t", [128, 32], F32))
    si_t2 = g1.enter_context(nc.sbuf_tensor("si_t2", [128, 32], F32))

    lws = []
    for i, (dst, srcap) in enumerate([(k_convw, convwd), (k_convb, convbd), (k_cp2, cpd),
                                      (k_dp2, dpd), (k_dtb, dtbd), (k_mask, maskd)]):
        lws.append(S.load(dst[:, :], srcap, slot=f"k{i}"))
    for h in range(2):
        lws.append(S.load(k_dtw[h][:, :], dtwd[h], slot=f"kdtw{h}"))
        lws.append(S.load(k_bp[h][:, :], bpd[h], slot=f"kbp{h}"))
    w_kload = lws[-1]
    for buf, val in [(zerot, 0.0), (p_fcar, 0.0), (p_dcar, 1.0), (p_sin, 0.0),
                     (k_ones_bf, 1.0), (k_ones_row, 1.0)]:
        S.emit("gpsimd", lambda g, b=buf, v=val: g.memset(b[:, :], v).then_inc(sem["sLg"], 1))
        S.inc("sLg", 1)
    w_memset = ("sLg", S.cur("sLg"))

    w_cp2 = C(lambda c: c.mul(out=k_cp2[:, :], in_=k_cp2[:, :], mul=0.5),
              waits=[lws[2], w_memset], inc=True)
    w_dp2 = C(lambda c: c.mul(out=k_dp2[:, :], in_=k_dp2[:, :], mul=0.25),
              waits=[lws[3]], inc=True)
    w_dtwh = []
    for h in range(2):
        wd = C(lambda c, h=h: c.mul(out=k_dtw[h][:, :], in_=k_dtw[h][:, :], mul=0.5),
               waits=[lws[6 + 2 * h]], inc=True)
        wb_ = C(lambda c, h=h: c.mul(out=k_bp[h][:, :], in_=k_bp[h][:, :], mul=0.5),
                waits=[lws[7 + 2 * h]], inc=True)
        w_dtwh.append((wd, wb_))

    sub = ExitStack()
    m_tmp = sub.enter_context(nc.sbuf_tensor("m_tmp", [32, 2048], F32))
    k_msk = sub.enter_context(nc.sbuf_tensor("k_msk", [32, 2048], F32))

    w_mdec = []
    prev_v = None
    for h in range(2):
        la = S.load(m_tmp[:, :], ald[h], waits=[prev_v] if prev_v else (), slot="mt")
        lm = S.load(k_msk[:, :], mskd[h], waits=[prev_v] if prev_v else (), slot="mk")
        we = C(lambda c: c.activation(out=m_tmp[:, :], in_=m_tmp[:, :], func=AF.Exp),
               waits=[la], inc=True)
        wv = V(lambda v, h=h: v.scalar_tensor_tensor(out=k_mdec[h][:, :], in0=m_tmp[:, :], scalar=-1.0, in1=k_msk[:, :], op0=ALU.mult, op1=ALU.mult),
               waits=[we, lm], inc=True)
        w_mdec.append(wv)
        prev_v = wv

    S.barrier()
    sub.close()

    a_x = g1.enter_context(nc.sbuf_tensor("a_x", [128, 16 * TT], BF16))
    a_ln = g1.enter_context(nc.sbuf_tensor("a_ln", [128, 16 * TT], BF16))
    a_X = [g1.enter_context(nc.sbuf_tensor(f"a_X{i}", [128, 16 * TT], BF16)) for i in range(2)]
    a_y0 = [g1.enter_context(nc.sbuf_tensor(f"a_y0{i}", [128, TT], BF16)) for i in range(2)]
    a_ag = [g1.enter_context(nc.sbuf_tensor(f"a_ag{i}", [128, TT], BF16)) for i in range(2)]
    a_pj = [g1.enter_context(nc.sbuf_tensor(f"a_pj{i}", [128, TT + 4], BF16)) for i in range(2)]
    a_acc = g1.enter_context(nc.sbuf_tensor("a_acc", [128, TT], BF16))
    a_tn = g1.enter_context(nc.sbuf_tensor("a_tn", [128, TT], BF16))
    a_t1 = g1.enter_context(nc.sbuf_tensor("a_t1", [128, TT], BF16))
    f_t = g1.enter_context(nc.sbuf_tensor("f_t", [128, TT], F32))
    f_dec = [g1.enter_context(nc.sbuf_tensor(f"f_dec{i}", [128, TT], F32)) for i in range(2)]
    f_drv = [g1.enter_context(nc.sbuf_tensor(f"f_drv{i}", [128, TT], F32)) for i in range(2)]
    f_st = [g1.enter_context(nc.sbuf_tensor(f"f_st{i}", [128, TT], F32)) for i in range(2)]
    f_ca = [g1.enter_context(nc.sbuf_tensor(f"f_ca{i}", [128, TT], F32)) for i in range(2)]
    f_sq = g1.enter_context(nc.sbuf_tensor("f_sq", [128, 4 * TT], BF16))
    d_zb = g1.enter_context(nc.sbuf_tensor("d_zb", [32, TT], F32))
    d_u = g1.enter_context(nc.sbuf_tensor("d_u", [32, TT], F32))
    d_y = g1.enter_context(nc.sbuf_tensor("d_y", [32, TT], F32))
    d_t = g1.enter_context(nc.sbuf_tensor("d_t", [32, TT], F32))
    d_bf = [g1.enter_context(nc.sbuf_tensor(f"d_bf{i}", [32, TT], BF16)) for i in range(2)]
    r_mean = g1.enter_context(nc.sbuf_tensor("r_mean", [1, TT], F32))
    r_msq = g1.enter_context(nc.sbuf_tensor("r_msq", [1, TT], F32))
    r_var = g1.enter_context(nc.sbuf_tensor("r_var", [1, TT], F32))
    r_r = g1.enter_context(nc.sbuf_tensor("r_r", [1, TT], F32))
    r_t = g1.enter_context(nc.sbuf_tensor("r_t", [1, TT], F32))
    stash = g1.enter_context(nc.sbuf_tensor("stash", [128, 48], BF16))
    pp = [g1.enter_context(nc.psum_tensor(f"pp{i}", [128, TT], F32)) for i in range(2)]
    pdt = g1.enter_context(nc.psum_tensor("pdt", [32, TT], F32))
    pdec = g1.enter_context(nc.psum_tensor("pdec", [128, TT], F32))
    pdrv = g1.enter_context(nc.psum_tensor("pdrv", [128, TT], F32))
    pst = g1.enter_context(nc.psum_tensor("pst", [128, TT], F32))
    psq = g1.enter_context(nc.psum_tensor("psq", [128, TT], F32))
    pbc = g1.enter_context(nc.psum_tensor("pbc", [128, TT], F32))

    S.emit("gpsimd", lambda g: g.memset(stash[:, :], 0.0).then_inc(sem["sLg"], 1))
    w_stash0 = ("sLg", S.inc("sLg", 1))

    def emit_stats(width, src_fn, extra_wait=None, nsrc=16):
        sl = slice(0, width)
        w_sum = S.mm(lambda: pst[0:1, sl],
                     [(lambda: k_ones_bf[:, :], (lambda dt=dt: src_fn(dt))) for dt in range(nsrc)],
                     waits=[stats_w[0], extra_wait, p2rel[0]])
        w_sumq = None
        for half in range(4):
            w_sql = None
            for i in range(4):
                dt = half * 4 + i
                w_sql = C(lambda c, dt=dt, i=i: c.activation(out=f_sq[:, i * TT:i * TT + width], in_=src_fn(dt), func=AF.Square),
                          waits=([stats_w[1], w_sumq, extra_wait] if i == 0 else ()), inc=(i == 3))
            w_sumq = S.mm(lambda: psq[0:1, sl],
                          [(lambda: k_ones_bf[:, :], (lambda i=i: f_sq[:, i * TT:i * TT + width])) for i in range(4)],
                          waits=[w_sql, stats_w[0] if half == 0 else None,
                                 bc_w[1] if half == 0 else None,
                                 p2rel[1] if half == 0 else None],
                          start_first=(half == 0), stop_last=(half == 3))
        ch_ = True
        V(lambda v: v.tensor_scalar(out=r_mean[:, sl], in0=pst[0:1, sl], scalar1=1.0 / D_, scalar2=None, op0=ALU.mult), waits=[w_sum], chain=ch_)
        V(lambda v: v.tensor_scalar(out=r_msq[:, sl], in0=psq[0:1, sl], scalar1=1.0 / D_, scalar2=None, op0=ALU.mult), waits=[w_sumq], chain=ch_)
        V(lambda v: v.tensor_tensor(out=r_t[:, sl], in0=r_mean[:, sl], in1=r_mean[:, sl], op=ALU.mult), chain=ch_)
        V(lambda v: v.tensor_tensor(out=r_var[:, sl], in0=r_msq[:, sl], in1=r_t[:, sl], op=ALU.subtract), chain=ch_)
        w_stat = V(lambda v: v.tensor_scalar(out=r_var[:, sl], in0=r_var[:, sl], scalar1=EPS, scalar2=None, op0=ALU.add), inc=True, chain=ch_)
        stats_w[0] = w_stat
        stats_w[1] = w_sumq
        S.rsqrt(r_r[:, sl], r_msq[:, sl], r_var[:, sl], chain=ch_)
        w_mr = V(lambda v: v.tensor_tensor(out=r_t[:, sl], in0=r_mean[:, sl], in1=r_r[:, sl], op=ALU.mult), inc=True, chain=ch_)
        w_bcr = S.mm(lambda: pbc[:, sl], [(lambda: k_ones_row[:, :], lambda: r_r[0:1, sl])],
                     waits=[w_mr, bc_w[0]])
        w_bcm = S.mm(lambda: psq[:, sl], [(lambda: k_ones_row[:, :], lambda: r_t[0:1, sl])],
                     waits=[w_mr, w_stat, bc_w[1]])
        return w_bcr, w_bcm

    stats_w = [w_memset, w_memset]
    bc_w = [None, None]
    p2rel = [None, None]

    def prefetch_w(srcap, base, nchunk, csz, slot):
        wtc = []
        for ckk in range(nchunk):
            wtc.append(S.load(wAB[:, ckk * csz:(ckk + 1) * csz],
                              srcap[:, base + ckk * csz:base + (ckk + 1) * csz],
                              waits=[wAB_rel[0]] if ckk == 0 else (), slot=slot))
        return wtc

    def stage_A(h, wtc):
        nonlocal a_ln_w
        mdec = k_mdec[h]
        kbp = k_bp[h]
        kdtw = k_dtw[h]

        pp_w = [None, None]
        st_w = [[], []]
        pdec_w = [None, None]
        pdrv_w = [None, None]
        pdt_w = [None]

        def lhsT_in(dt, it):
            return wAB[:, (it * 16 + dt) * 128:(it * 16 + dt) * 128 + 128]

        if h == 0:
            S.emit("gpsimd", lambda g: g.memset(a_ln[:, :], 0.0).then_inc(sem["sLg"], 1))
            w_amem = ("sLg", S.inc("sLg", 1))
            a_ln_w.append(w_amem)
            xb3 = S.load(a_x[:, 0:64].rearrange("p (a b) -> p a b", a=16),
                         xTd.rearrange("p (a b) -> p a b", a=16)[:, :, 0:4],
                         waits=[w_memset], slot="ax")
            w_bcr, w_bcm = emit_stats(3, lambda dt: a_x[:, dt * 4:dt * 4 + 3], extra_wait=xb3)
            w_ln3 = None
            first_ln_waits = [w_amem]
            for dt in range(ND):
                V(lambda v, dt=dt: v.tensor_tensor(out=f_t[:, 0:3], in0=a_x[:, dt * 4:dt * 4 + 3], in1=pbc[:, 0:3], op=ALU.mult),
                  waits=([w_bcr, w_bcm] + first_ln_waits) if dt == 0 else (), chain=True)
                w_ln3 = V(lambda v, dt=dt: v.tensor_tensor(out=a_ln[:, dt * TT:dt * TT + 3], in0=f_t[:, 0:3], in1=psq[:, 0:3], op=ALU.subtract),
                          inc=(dt == ND - 1), chain=True)
            bc_w[0] = w_ln3
            bc_w[1] = w_ln3
            rst = S.store(sp_lnx.rearrange("p (a b) -> p a b", a=16)[:, :, 0:4],
                          a_ln[:, :].rearrange("p (a b) -> p a b", a=16)[:, :, 0:4], waits=[w_ln3], slot="slnx")
        else:
            xb3 = S.load(a_ln[:, :].rearrange("p (a b) -> p a b", a=16)[:, :, 0:4],
                         sp_lnx.rearrange("p (a b) -> p a b", a=16)[:, :, 0:4],
                         waits=a_ln_w, slot="aln")
            w_ln3 = xb3
        for it in range(NIT):
            w = S.mm(lambda: pp[0][:, 0:3],
                     [((lambda dt=dt, it=it: lhsT_in(dt, it)), (lambda dt=dt: a_ln[:, dt * TT:dt * TT + 3])) for dt in range(ND)],
                     waits=[w_ln3, wtc[it // 4], pp_w[0], w_stash0])
            pp_w[0] = V(lambda v, it=it: v.tensor_copy(out=stash[:, it * 3:it * 3 + 3], in_=pp[0][:, 0:3]),
                        waits=[w], inc=True)
        a_ln_w = [("sTm", S.cur("sTm"))] + ([rst] if h == 0 else [])
        if h == 0:
            a_x_w[:] = [w_ln3, ("sSc", S.cur("sSc")), ("sTm", S.cur("sTm"))]

        TST = [None] * NTILE

        def phase1(ti):
            nonlocal a_ln_w
            st = {}
            xb = ti % 2
            t0 = ti * TT
            if h == 0:
                xw = S.load(a_x[:, :].rearrange("p (a b) -> p a b", a=16),
                            xTd.rearrange("p (a b) -> p a b", a=16)[:, :, 3 + t0:3 + t0 + TT],
                            waits=a_x_w, slot="ax")
                w_bcr, w_bcm = emit_stats(TT, lambda dt: a_x[:, dt * TT:(dt + 1) * TT],
                                          extra_wait=xw)
                w_sq_end = ("sSc", S.cur("sSc"))
                w_stats_mm = ("sTm", S.cur("sTm"))
                w_lnx = None
                for dt in range(ND):
                    V(lambda v, dt=dt: v.tensor_tensor(out=f_t[:, :], in0=a_x[:, dt * TT:(dt + 1) * TT], in1=pbc[:, :], op=ALU.mult),
                      waits=([w_bcr, w_bcm] + a_ln_w) if dt == 0 else ())
                    w_lnx = V(lambda v, dt=dt: v.tensor_tensor(out=a_ln[:, dt * TT:(dt + 1) * TT], in0=f_t[:, :], in1=psq[:, :], op=ALU.subtract),
                              inc=(dt == ND - 1))
                bc_w[0] = w_lnx
                bc_w[1] = w_lnx

                a_x_w[:] = [w_lnx, w_sq_end, w_stats_mm]
                tst = S.store(sp_lnx.rearrange("p (a b) -> p a b", a=16)[:, :, 3 + t0:3 + t0 + TT],
                              a_ln[:, :].rearrange("p (a b) -> p a b", a=16), waits=[w_lnx], slot="slnx")
                a_ln_next = [tst]
            else:
                w_lnx = S.load(a_ln[:, :].rearrange("p (a b) -> p a b", a=16),
                               sp_lnx.rearrange("p (a b) -> p a b", a=16)[:, :, 3 + t0:3 + t0 + TT],
                               waits=a_ln_w, slot="aln")
                a_ln_next = []

            for it in range(NIT):
                pb = it % 2
                kt = h * 16 + it
                w_mm = S.mm(lambda pb=pb: pp[pb][:, :],
                            [((lambda dt=dt, it=it: lhsT_in(dt, it)),
                              (lambda dt=dt: a_ln[:, dt * TT:(dt + 1) * TT])) for dt in range(ND)],
                            waits=[w_lnx, wtc[it // 4] if ti == 0 else None, pp_w[pb]])
                w_pe = C(lambda c, pb=pb: c.activation(out=a_pj[pb][:, 4:TT + 4], in_=pp[pb][:, :], func=AF.Copy),
                         waits=[w_mm, pj_rel[pb]], inc=True)
                pp_w[pb] = w_pe
                V(lambda v, pb=pb, it=it: v.tensor_copy(out=a_pj[pb][:, 1:4], in_=stash[:, it * 3:it * 3 + 3]),
                  waits=[w_pe], chain=True)
                V(lambda v, pb=pb, kt=kt: v.tensor_scalar(out=a_acc[:, :], in0=a_pj[pb][:, 1:TT + 1],
                                                          scalar1=k_convw[:, 4 * kt:4 * kt + 1],
                                                          scalar2=k_convb[:, kt:kt + 1],
                                                          op0=ALU.mult, op1=ALU.add),
                  waits=[acc_rel[0]])
                V(lambda v, pb=pb, kt=kt: v.scalar_tensor_tensor(out=a_acc[:, :], in0=a_pj[pb][:, 2:TT + 2],
                                                                 scalar=k_convw[:, 4 * kt + 1:4 * kt + 2],
                                                                 in1=a_acc[:, :], op0=ALU.mult, op1=ALU.add))
                V(lambda v, pb=pb, kt=kt: v.scalar_tensor_tensor(out=a_acc[:, :], in0=a_pj[pb][:, 3:TT + 3],
                                                                 scalar=k_convw[:, 4 * kt + 2:4 * kt + 3],
                                                                 in1=a_acc[:, :], op0=ALU.mult, op1=ALU.add))
                w_acc = V(lambda v, pb=pb, kt=kt: v.scalar_tensor_tensor(out=a_acc[:, :], in0=a_pj[pb][:, 4:TT + 4],
                                                                         scalar=k_convw[:, 4 * kt + 3:4 * kt + 4],
                                                                         in1=a_acc[:, :], op0=ALU.mult, op1=ALU.add), inc=True)
                w_stash = V(lambda v, pb=pb, it=it: v.tensor_copy(out=stash[:, it * 3:it * 3 + 3], in_=a_pj[pb][:, TT + 1:TT + 4]), inc=True)
                pj_rel[pb] = w_stash
                w_tanh = C(lambda c: c.activation(out=a_tn[:, :], in_=a_acc[:, :], func=AF.Tanh, scale=0.5),
                           waits=[w_acc, silu_rel[0]], inc=True)
                w_X = V(lambda v, it=it, xb=xb: v.scalar_tensor_tensor(out=a_X[xb][:, it * TT:(it + 1) * TT],
                                                                       in0=a_tn[:, :], scalar=1.0, in1=a_acc[:, :],
                                                                       op0=ALU.add, op1=ALU.mult),
                        waits=[w_tanh] + (aX_rel[xb] if it == 0 else []), inc=True)
                acc_rel[0] = w_X
                silu_rel[0] = w_X
            a_ln_w = [("sTm", S.cur("sTm"))] + a_ln_next

            w_dt = S.mm(lambda: pdt[0:32, :],
                        [((lambda it=it: kdtw[:, it * 32:(it + 1) * 32]),
                          (lambda it=it, xb=xb: a_X[xb][:, it * TT:(it + 1) * TT])) for it in range(NIT)],
                        waits=[w_X, pdt_w[0], w_dtwh[h][0] if ti == 0 else None])
            V(lambda v: v.tensor_copy(out=d_zb[:, :], in_=pdt[0:32, :]),
              waits=[w_dt, dtb_rel[0]])
            V(lambda v, h=h: v.tensor_scalar(out=d_zb[:, :], in0=d_zb[:, :], scalar1=k_dtb[:, h:h + 1], scalar2=None, op0=ALU.add))
            w_zb = V(lambda v: v.tensor_copy(out=d_t[:, :], in_=d_zb[:, :]), inc=True)
            pdt_w[0] = w_zb
            C(lambda c: c.activation(out=d_u[:, :], in_=d_t[:, :], func=AF.Abs), waits=[w_zb])
            w_u = C(lambda c: c.activation(out=d_u[:, :], in_=d_u[:, :], func=AF.Exp, scale=-1.0), inc=True)
            V(lambda v: v.tensor_scalar(out=d_y[:, :], in0=d_u[:, :], scalar1=-0.25, scalar2=1.0 / 3.0, op0=ALU.mult, op1=ALU.add), waits=[w_u])
            V(lambda v: v.tensor_tensor(out=d_y[:, :], in0=d_y[:, :], in1=d_u[:, :], op=ALU.mult))
            V(lambda v: v.tensor_scalar(out=d_y[:, :], in0=d_y[:, :], scalar1=-0.5, scalar2=None, op0=ALU.add))
            V(lambda v: v.tensor_tensor(out=d_y[:, :], in0=d_y[:, :], in1=d_u[:, :], op=ALU.mult))
            V(lambda v: v.tensor_scalar(out=d_y[:, :], in0=d_y[:, :], scalar1=1.0, scalar2=None, op0=ALU.add))
            w_ny = V(lambda v: v.tensor_tensor(out=d_y[:, :], in0=d_y[:, :], in1=d_u[:, :], op=ALU.mult), inc=True)
            V(lambda v: v.tensor_scalar(out=d_u[:, :], in0=d_u[:, :], scalar1=1.0, scalar2=None, op0=ALU.add))
            for _ in range(2):
                w_e = C(lambda c: c.activation(out=d_t[:, :], in_=d_y[:, :], func=AF.Exp, scale=-1.0), waits=[w_ny], inc=True)
                V(lambda v: v.tensor_tensor(out=d_t[:, :], in0=d_t[:, :], in1=d_u[:, :], op=ALU.mult), waits=[w_e])
                w_ny = V(lambda v: v.scalar_tensor_tensor(out=d_y[:, :], in0=d_t[:, :], scalar=-1.0, in1=d_y[:, :], op0=ALU.add, op1=ALU.add), inc=True)
            V(lambda v: v.tensor_scalar(out=d_t[:, :], in0=d_zb[:, :], scalar1=0.0, scalar2=None, op0=ALU.max))
            V(lambda v: v.tensor_tensor(out=d_y[:, :], in0=d_t[:, :], in1=d_y[:, :], op=ALU.add))
            w_dtbf = V(lambda v, xb=xb: v.tensor_copy(out=d_bf[xb][:, :], in_=d_y[:, :]), waits=dbf_rel[xb], inc=True)
            dtb_rel[0] = w_dtbf
            st["w_dtbf"] = w_dtbf
            st["t0"] = t0
            st["xb"] = xb
            TST[ti] = st

        def phase2(ti):
            st = TST[ti]
            xb = st["xb"]
            t0 = st["t0"]
            w_dtbf = st["w_dtbf"]
            w_vm_last = None
            w_y0 = None
            for kk in range(NIT):
                k_g = h * 16 + kk
                sb = kk % 2
                pdecb = [pdec, pst][sb]
                pdrvb = [pdrv, psq][sb]
                w_dm = S.mm(lambda pdecb=pdecb: pdecb[:, :],
                            [((lambda kk=kk: mdec[:, kk * 128:(kk + 1) * 128]), (lambda xb=xb: d_bf[xb][:, :]))],
                            waits=[w_dtbf, pdec_w[sb], stats_w[0] if (sb == 1 and kk == 1) else None,
                                   w_mdec[h] if (ti == 0 and kk == 0) else None])
                w_vm = S.mm(lambda pdrvb=pdrvb: pdrvb[:, :],
                            [((lambda kk=kk: kbp[:, kk * 128:(kk + 1) * 128]), (lambda xb=xb: d_bf[xb][:, :]))],
                            waits=[pdrv_w[sb], bc_w[1] if (sb == 1 and kk == 1) else None,
                                   w_dtwh[h][1] if (ti == 0 and kk == 0) else None])
                w_vm_last = w_vm
                w_dec = C(lambda c, sb=sb, pdecb=pdecb: c.activation(out=f_dec[sb][:, :], in_=pdecb[:, :], func=AF.Exp),
                          waits=[w_dm, dec_rel[sb]], inc=True)
                pdec_w[sb] = w_dec
                w_drv = V(lambda v, kk=kk, xb=xb, sb=sb, pdrvb=pdrvb: v.tensor_tensor(out=f_drv[sb][:, :], in0=pdrvb[:, :], in1=a_X[xb][:, kk * TT:(kk + 1) * TT], op=ALU.mult),
                          waits=[w_vm], inc=True)
                pdrv_w[sb] = w_drv
                V(lambda v, k_g=k_g, sb=sb: v.tensor_tensor_scan(out=f_st[sb][:, :], data0=f_dec[sb][:, :], data1=f_drv[sb][:, :],
                                                                 initial=p_fcar[:, k_g:k_g + 1], op0=ALU.mult, op1=ALU.add),
                  waits=[w_dec, st_rel[sb]])
                V(lambda v, k_g=k_g, sb=sb: v.tensor_tensor_scan(out=f_ca[sb][:, :], data0=f_dec[sb][:, :], data1=zerot[:, :],
                                                                 initial=p_dcar[:, k_g:k_g + 1], op0=ALU.mult, op1=ALU.add),
                  waits=[ca_rel[sb]])
                V(lambda v, k_g=k_g, sb=sb: v.tensor_copy(out=p_fcar[:, k_g:k_g + 1], in_=f_st[sb][:, TT - 1:TT]))
                w_car = V(lambda v, k_g=k_g, sb=sb: v.tensor_copy(out=p_dcar[:, k_g:k_g + 1], in_=f_ca[sb][:, TT - 1:TT]), inc=True)
                dec_rel[sb] = w_car
                ob = kk % 2
                w_t1 = C(lambda c, k_g=k_g, sb=sb: c.activation(out=a_t1[:, :], in_=f_st[sb][:, :], func=AF.Copy, scale=k_cp2[:, k_g:k_g + 1]),
                         waits=[w_car, w_cp2 if (ti == 0 and kk == 0) else None, t1_rel[0]], inc=True)
                st_rel[sb] = w_t1
                w_ag = C(lambda c, k_g=k_g, ob=ob, sb=sb: c.activation(out=a_ag[ob][:, :], in_=f_ca[sb][:, :], func=AF.Copy, scale=k_cp2[:, k_g:k_g + 1]),
                         waits=(st_w[ob] or []) + [w_dp2 if (ti == 0 and kk == 0) else None], inc=True)
                ca_rel[sb] = w_ag
                w_y0 = V(lambda v, k_g=k_g, kk=kk, ob=ob, xb=xb: v.scalar_tensor_tensor(out=a_y0[ob][:, :], in0=a_X[xb][:, kk * TT:(kk + 1) * TT],
                                                                                        scalar=k_dp2[:, k_g:k_g + 1], in1=a_t1[:, :],
                                                                                        op0=ALU.mult, op1=ALU.add),
                         waits=[w_t1] + (st_w[ob] or []), inc=True)
                t1_rel[0] = w_y0
                st1 = S.store(sp_y0.rearrange("p (a b) -> p a b", a=32)[:, k_g, t0:t0 + TT], a_y0[ob][:, :], waits=[w_y0], slot=f"sy{ob}")
                st2 = S.store(sp_ag.rearrange("p (a b) -> p a b", a=32)[:, k_g, t0:t0 + TT], a_ag[ob][:, :], waits=[w_ag], slot=f"sa{ob}")
                st_w[ob] = [st1, st2]
            aX_rel[xb][:] = [w_vm_last, w_y0]
            dbf_rel[xb][:] = [w_vm_last]
            p2rel[0] = pdec_w[1]
            p2rel[1] = pdrv_w[1]

        phase1(0)
        for ti in range(NTILE):
            if ti + 1 < NTILE:
                phase1(ti + 1)
            phase2(ti)
        wAB_rel[0] = ("sTm", S.cur("sTm"))

    pj_rel = [None, None]
    acc_rel = [None]
    silu_rel = [None]
    t1_rel = [None]
    aX_rel = [[], []]
    dtb_rel = [None]
    dbf_rel = [[], []]
    dec_rel = [None, None]
    drv_rel = [None]
    st_rel = [None, None]
    ca_rel = [None, None]
    a_ln_w = []
    a_x_w = [w_memset]

    if stage_limit >= 1:
        wtcA0 = prefetch_w(ind, 0, 4, 8192, "wA")
        stage_A(0, wtcA0)
        a_ln_w = [("sTm", S.cur("sTm"))]
        wtcA1 = prefetch_w(ind, TILEW, 4, 8192, "wA")
        S.barrier()
    if stage_limit >= 2:
        stage_A(1, wtcA1)
        S.barrier()

    if stage_limit >= 3:
        w_cc1 = V(lambda v: v.tensor_copy(out=cc_sb[:, 0:32], in_=p_fcar[:, :]))
        w_cc = V(lambda v: v.tensor_copy(out=cc_sb[:, 32:64], in_=p_dcar[:, :]), inc=True)
        stc = S.store(cc_in, cc_sb[:, :], waits=[w_cc], slot="scc")
        S.wait("gpsimd", stc)
        S.emit("gpsimd", lambda g: g.collective_compute(
            "AllGather", ALU.bypass, replica_groups=[list(range(NCORES))],
            ins=[cc_in.opt()], outs=[cc_out.opt()]).then_inc(sem["sCC"]))
        S.inc("sCC", 1)
        S.emit("gpsimd", lambda g: g.wait_ge(sem["sCC"], 1))
        gl = None
        for r in range(NCORES):
            gl = S.load(cc_g[:, 64 * r:64 * r + 64], cc_out[128 * r:128 * (r + 1), :], slot="ccg")
        for r in range(NCORES):
            V(lambda v, r=r: v.tensor_tensor(out=si_t[:, :], in0=cc_g[:, 64 * r + 32:64 * r + 64], in1=p_sin[:, :], op=ALU.mult),
              waits=[gl, w_memset] if r == 0 else (), chain=True)
            V(lambda v, r=r: v.tensor_tensor(out=si_t[:, :], in0=si_t[:, :], in1=cc_g[:, 64 * r:64 * r + 32], op=ALU.add), chain=True)
            V(lambda v: v.tensor_tensor(out=si_t2[:, :], in0=si_t[:, :], in1=p_sin[:, :], op=ALU.subtract), chain=True)
            V(lambda v, r=r: v.scalar_tensor_tensor(out=p_sin[:, :], in0=si_t2[:, :], scalar=k_mask[:, r:r + 1], in1=p_sin[:, :], op0=ALU.mult, op1=ALU.add), chain=True)
        wtcB0 = prefetch_w(gated, 0, 4, 8192, "wB")
        S.barrier()
    g1.close()

    g2 = ExitStack()
    b_ln = [g2.enter_context(nc.sbuf_tensor(f"b_ln{i}", [128, 16 * TT], BF16)) for i in range(2)]
    b_y0 = [g2.enter_context(nc.sbuf_tensor(f"b_y0{i}", [128, TT], BF16)) for i in range(4)]
    b_ag = [g2.enter_context(nc.sbuf_tensor(f"b_ag{i}", [128, TT], BF16)) for i in range(4)]
    b_t = g2.enter_context(nc.sbuf_tensor("b_t", [128, TT], F32))
    b_y = g2.enter_context(nc.sbuf_tensor("b_y", [128, TT], F32))
    b_cb = [g2.enter_context(nc.sbuf_tensor(f"b_cb{i}", [128, TT], BF16)) for i in range(4)]
    pg = [g2.enter_context(nc.psum_tensor(f"pg{i}", [128, TT], F32)) for i in range(2)]

    def stage_B(h, wtc):
        pg_w = [None, None]
        cb_w = [None, None, None, None]
        ln_w = [None, None]
        bt_w = None
        for ti in range(NTILE):
            t0 = ti * TT
            lb = ti % 2
            lnw = S.load(b_ln[lb][:, :].rearrange("p (a b) -> p a b", a=16),
                         sp_lnx.rearrange("p (a b) -> p a b", a=16)[:, :, 3 + t0:3 + t0 + TT],
                         waits=[ln_w[lb]], slot=f"bln{lb}")
            for kk in range(NIT):
                k_g = h * 16 + kk
                ob = kk % 4
                ly = S.load(b_y0[ob][:, :], sp_y0.rearrange("p (a b) -> p a b", a=32)[:, k_g, t0:t0 + TT],
                            waits=[cb_w[ob]], slot=f"by{ob}")
                la = S.load(b_ag[ob][:, :], sp_ag.rearrange("p (a b) -> p a b", a=32)[:, k_g, t0:t0 + TT],
                            waits=[cb_w[ob]], slot=f"ba{ob}")
                w_mm = S.mm(lambda ob=ob: pg[ob % 2][:, :],
                            [((lambda dt=dt, kk=kk: wAB[:, (kk * 16 + dt) * 128:(kk * 16 + dt) * 128 + 128]),
                              (lambda dt=dt, lb=lb: b_ln[lb][:, dt * TT:(dt + 1) * TT])) for dt in range(ND)],
                            waits=[lnw, wtc[kk // 4] if ti == 0 else None, pg_w[ob % 2]])
                w_tanh = C(lambda c, ob=ob: c.activation(out=b_t[:, :], in_=pg[ob % 2][:, :], func=AF.Tanh, scale=0.5),
                           waits=[w_mm, bt_w], inc=True)
                pg_w[ob % 2] = w_tanh
                w_y = V(lambda v, k_g=k_g, ob=ob: v.scalar_tensor_tensor(out=b_y[:, :], in0=b_ag[ob][:, :],
                                                                         scalar=p_sin[:, k_g:k_g + 1], in1=b_y0[ob][:, :],
                                                                         op0=ALU.mult, op1=ALU.add),
                        waits=[la, ly], inc=True)
                w_cb = V(lambda v, ob=ob: v.scalar_tensor_tensor(out=b_cb[ob][:, :], in0=b_t[:, :], scalar=1.0,
                                                                 in1=b_y[:, :], op0=ALU.add, op1=ALU.mult),
                         waits=[w_tanh, cb_w[ob]], inc=True)
                bt_w = w_cb
                st = S.store(sp_comb.rearrange("p (a b) -> p a b", a=32)[:, k_g, t0:t0 + TT], b_cb[ob][:, :], waits=[w_cb], slot=f"scb{ob}")
                cb_w[ob] = st
            ln_w[lb] = ("sTm", S.cur("sTm"))
        wAB_rel[0] = ("sTm", S.cur("sTm"))

    if stage_limit >= 4:
        stage_B(0, wtcB0)
        wtcB1 = prefetch_w(gated, TILEW, 4, 8192, "wB")
        S.barrier()
        stage_B(1, wtcB1)
        wtcC0 = prefetch_w(outPd, 0, 2, 16384, "wC")
        S.barrier()
    g2.close()

    g3 = ExitStack()
    c_cb = [g3.enter_context(nc.sbuf_tensor(f"c_cb{i}", [128, 32 * TC], BF16)) for i in range(2)]
    c_mo = g3.enter_context(nc.sbuf_tensor("c_mo", [128, 1024], BF16))
    c_mo1 = [g3.enter_context(nc.sbuf_tensor(f"c_mo1{i}", [128, 1024], BF16)) for i in range(2)]
    c_xt = [g3.enter_context(nc.sbuf_tensor(f"c_xt{i}", [128, 2048], F32)) for i in range(2)]
    c_x1 = g3.enter_context(nc.sbuf_tensor("c_x1", [128, 2048], F32))
    c_ln1 = g3.enter_context(nc.sbuf_tensor("c_ln1", [128, 2048], BF16))
    c_sq = g3.enter_context(nc.sbuf_tensor("c_sq", [128, 1024], F32))
    c_st = g3.enter_context(nc.sbuf_tensor("c_st", [128, 2], F32))
    c_stp = [g3.enter_context(nc.sbuf_tensor(f"c_stp{i}", [128, 2], F32)) for i in range(2)]
    c_m = g3.enter_context(nc.sbuf_tensor("c_m", [128, 1], F32))
    c_v = g3.enter_context(nc.sbuf_tensor("c_v", [128, 1], F32))
    c_r = g3.enter_context(nc.sbuf_tensor("c_r", [128, 1], F32))
    c_w1 = g3.enter_context(nc.sbuf_tensor("c_w1", [128, 1], F32))
    c_w2 = g3.enter_context(nc.sbuf_tensor("c_w2", [128, 1], F32))
    c_tt = g3.enter_context(nc.sbuf_tensor("c_tt", [128, 1024], F32))
    pc = [g3.enter_context(nc.psum_tensor(f"pc{i}", [128, 1024], F32)) for i in range(2)]
    k_h64 = g3.enter_context(nc.sbuf_tensor("k_h64", [64, 64], BF16))
    fo_rhs = [g3.enter_context(nc.sbuf_tensor(f"fo_rhs{i}", [64, 2048], BF16)) for i in range(4)]
    fo_out = [g3.enter_context(nc.sbuf_tensor(f"fo_out{i}", [64, 2048], BF16)) for i in range(4)]
    pfold = [g3.enter_context(nc.psum_tensor(f"pfold{i}", [64, 512], F32)) for i in range(4)]
    fold_state = {"j4": 0, "sc": [None] * 4, "st": [None] * 4, "psc": [None] * 4,
                  "h64": None}

    def emit_fold_group():
        j4 = fold_state["j4"]
        if j4 >= 128:
            return
        fold_state["j4"] = j4 + 1
        if fold_state["h64"] is None:
            fold_state["h64"] = S.load(k_h64[:, :], h64d, slot="h64")
        b4 = j4 % 4
        fs = fold_state
        lt = S.load(fo_rhs[b4][:, :], w2ad[:, 2048 * j4:2048 * (j4 + 1)],
                    waits=[w for w in (fs["sc"][b4],) if w], slot=f"fo{b4}")
        wscs = None
        for jj in range(4):
            wm = S.mm(lambda jj=jj: pfold[jj][:, :],
                      [(lambda: k_h64[:, :], lambda b4=b4, jj=jj: fo_rhs[b4][:, jj * 512:(jj + 1) * 512])],
                      waits=[lt, fs["h64"] if j4 == 0 else None, fs["psc"][jj]])
            wscs = C(lambda c, b4=b4, jj=jj: c.activation(out=fo_out[b4][:, jj * 512:(jj + 1) * 512], in_=pfold[jj][:, :], func=AF.Copy, scale=RS8192),
                     waits=[wm, fs["st"][b4] if jj == 0 else None], inc=True)
            fs["psc"][jj] = wscs
        st = S.store(w2f[:, 2048 * j4:2048 * (j4 + 1)], fo_out[b4][:, :], waits=[wscs], slot=f"w2f{b4}")
        fs["st"][b4] = st
        fs["sc"][b4] = st

    NTC = CHUNK // TC

    def stage_C(dh, wtc):
        pc_w = [[], []]
        cb_ld = [None, None]
        mo_st = None
        st_st = None
        sq_w = None
        x1_w = None
        ln1_w = None
        xt_w = [None, None]
        mo1_w = [None, None]
        stp_w = [None, None]
        for ti in range(NTC):
            t0 = ti * TC
            lb = ti % 2
            lc = S.load(c_cb[lb][:, :].rearrange("p (a b) -> p a b", a=32),
                        sp_comb.rearrange("p (a b) -> p a b", a=32)[:, :, t0:t0 + TC],
                        waits=[cb_ld[lb]], slot=f"cb{lb}")
            for ts in range(2):
                if c_parts < 2:
                    continue
                tb = ti * 2 + ts
                pb = ts
                pairs_a = [((lambda ct=ct, ts=ts, lb=lb: c_cb[lb][:, ct * TC + ts * 128:ct * TC + ts * 128 + 128]),
                            (lambda ct=ct: wAB[:, ct * 512:ct * 512 + 512])) for ct in range(32)]
                pairs_b = [((lambda ct=ct, ts=ts, lb=lb: c_cb[lb][:, ct * TC + ts * 128:ct * TC + ts * 128 + 128]),
                            (lambda ct=ct: wAB[:, 16384 + ct * 512:16384 + ct * 512 + 512])) for ct in range(32)]
                w_a = S.mm(lambda pb=pb: pc[pb][:, 0:512], pairs_a,
                           waits=[lc, wtc[0] if tb == 0 else None] + (pc_w[pb] or []))
                w_b = S.mm(lambda pb=pb: pc[pb][:, 512:1024], pairs_b,
                           waits=[wtc[1] if tb == 0 else None])
                if c_stats:
                    w_sq = C(lambda c, pb=pb: c.activation(out=c_sq[:, :], in_=pc[pb][:, :], func=AF.Square),
                             waits=[w_a, w_b, sq_w], inc=True)
                    w_r1 = V(lambda v, pb=pb: v.tensor_reduce(out=c_st[:, 0:1], in_=pc[pb][:, :], axis=AX.X, op=ALU.add),
                             waits=[w_sq, st_st], chain=True)
                    w_red = V(lambda v: v.tensor_reduce(out=c_st[:, 1:2], in_=c_sq[:, :], axis=AX.X, op=ALU.add),
                              inc=True, chain=True)
                    sq_w = w_red
                else:
                    w_sq = None
                    w_r1 = [w_a, w_b]
                    w_red = ("sVe", None)
                if c_parts < 3:
                    pc_w[pb] = [w_a, w_b]
                    continue
                if dh == 0:
                    w_ev = C(lambda c, pb=pb: c.activation(out=c_mo[:, :], in_=pc[pb][:, :], func=AF.Copy),
                             waits=([mo_st, w_r1] if c_stats else [mo_st] + w_r1), inc=True)
                    pc_w[pb] = [w_ev]
                    for _fg in range(4):
                        emit_fold_group()
                    mo_st = S.store(sp_mo1[tb * 128:(tb + 1) * 128, :], c_mo[:, :], waits=[w_ev], slot="cmo")
                    if c_stats:
                        st_st = S.store(sp_st2[tb * 128:(tb + 1) * 128, :], c_st[:, :], waits=[w_red], slot="cst")
                else:
                    lm = S.load(c_mo1[pb][:, :], sp_mo1[tb * 128:(tb + 1) * 128, :], waits=[mo1_w[pb]], slot=f"cm{pb}")
                    lp = S.load(c_stp[pb][:, :], sp_st2[tb * 128:(tb + 1) * 128, :], waits=[stp_w[pb]], slot=f"cp{pb}")
                    lx = S.load(c_xt[pb][:, :], xtokd[tb * 128:(tb + 1) * 128, :], waits=[xt_w[pb]], slot=f"cx{pb}")
                    w_st = V(lambda v, pb=pb: v.tensor_tensor(out=c_st[:, :], in0=c_st[:, :], in1=c_stp[pb][:, :], op=ALU.add),
                             waits=[lp, w_red], inc=True, chain=True)
                    stp_w[pb] = w_st
                    st_st = w_st
                    V(lambda v: v.tensor_scalar(out=c_m[:, :], in0=c_st[:, 0:1], scalar1=1.0 / D_, scalar2=None, op0=ALU.mult), chain=True)
                    V(lambda v: v.tensor_scalar(out=c_v[:, :], in0=c_st[:, 1:2], scalar1=1.0 / D_, scalar2=None, op0=ALU.mult), chain=True)
                    V(lambda v: v.tensor_tensor(out=c_w1[:, :], in0=c_m[:, :], in1=c_m[:, :], op=ALU.mult), chain=True)
                    V(lambda v: v.tensor_tensor(out=c_v[:, :], in0=c_v[:, :], in1=c_w1[:, :], op=ALU.subtract), chain=True)
                    V(lambda v: v.tensor_scalar(out=c_v[:, :], in0=c_v[:, :], scalar1=EPS, scalar2=None, op0=ALU.add), chain=True)
                    S.rsqrt(c_r[:, 0:1], c_w1[:, 0:1], c_v[:, 0:1])
                    V(lambda v, pb=pb: v.tensor_scalar(out=c_tt[:, :], in0=c_mo1[pb][:, :], scalar1=c_m[:, 0:1], scalar2=c_r[:, 0:1], op0=ALU.subtract, op1=ALU.mult),
                      waits=[lm, x1_w, ln1_w])
                    w_p1 = V(lambda v, pb=pb: v.tensor_tensor(out=c_x1[:, 0:1024], in0=c_tt[:, :], in1=c_xt[pb][:, 0:1024], op=ALU.add),
                             waits=[lx], inc=True)
                    mo1_w[pb] = w_p1
                    V(lambda v, pb=pb: v.tensor_scalar(out=c_tt[:, :], in0=pc[pb][:, :], scalar1=c_m[:, 0:1], scalar2=c_r[:, 0:1], op0=ALU.subtract, op1=ALU.mult))
                    w_x1 = V(lambda v, pb=pb: v.tensor_tensor(out=c_x1[:, 1024:2048], in0=c_tt[:, :], in1=c_xt[pb][:, 1024:2048], op=ALU.add), inc=True)
                    pc_w[pb] = [w_x1]
                    xt_w[pb] = w_x1
                    w_sq3 = C(lambda c: c.activation(out=c_sq[:, :], in_=c_x1[:, 0:1024], func=AF.Square),
                              waits=[w_x1, sq_w], inc=True)
                    w_sq3b = C(lambda c: c.activation(out=c_tt[:, :], in_=c_x1[:, 1024:2048], func=AF.Square), inc=True)
                    V(lambda v: v.tensor_reduce(out=c_st[:, 0:1], in_=c_x1[:, :], axis=AX.X, op=ALU.add),
                      waits=[w_x1, w_st], chain=True)
                    V(lambda v: v.tensor_reduce(out=c_w1[:, :], in_=c_sq[:, :], axis=AX.X, op=ALU.add),
                      waits=[w_sq3], chain=True)
                    w_rq = V(lambda v: v.tensor_reduce(out=c_w2[:, :], in_=c_tt[:, :], axis=AX.X, op=ALU.add),
                             waits=[w_sq3b], inc=True, chain=True)
                    sq_w = w_rq
                    V(lambda v: v.tensor_tensor(out=c_w1[:, :], in0=c_w1[:, :], in1=c_w2[:, :], op=ALU.add), chain=True)
                    V(lambda v: v.tensor_scalar(out=c_m[:, :], in0=c_st[:, 0:1], scalar1=1.0 / D_, scalar2=None, op0=ALU.mult), chain=True)
                    V(lambda v: v.tensor_scalar(out=c_v[:, :], in0=c_w1[:, :], scalar1=1.0 / D_, scalar2=None, op0=ALU.mult), chain=True)
                    V(lambda v: v.tensor_tensor(out=c_w2[:, :], in0=c_m[:, :], in1=c_m[:, :], op=ALU.mult), chain=True)
                    V(lambda v: v.tensor_tensor(out=c_v[:, :], in0=c_v[:, :], in1=c_w2[:, :], op=ALU.subtract), chain=True)
                    V(lambda v: v.tensor_scalar(out=c_v[:, :], in0=c_v[:, :], scalar1=EPS, scalar2=None, op0=ALU.add), chain=True)
                    S.rsqrt(c_r[:, 0:1], c_w1[:, 0:1], c_v[:, 0:1])
                    w_ln1 = V(lambda v: v.tensor_scalar(out=c_ln1[:, :], in0=c_x1[:, :], scalar1=c_m[:, 0:1], scalar2=c_r[:, 0:1], op0=ALU.subtract, op1=ALU.mult),
                              waits=[ln1_w], inc=True)
                    st1 = S.store(sp_x1[tb * 128:(tb + 1) * 128, :], c_x1[:, :], waits=[w_ln1], slot="sx1")
                    st2 = S.store(sp_lnx1[tb * 128:(tb + 1) * 128, :], c_ln1[:, :], waits=[w_ln1], slot="sln1")
                    x1_w = st1
                    ln1_w = st2
                    st_st = w_st
            cb_ld[lb] = ("sTm", S.cur("sTm"))
        wAB_rel[0] = ("sTm", S.cur("sTm"))

    if stage_limit >= 5:
        stage_C(0, wtcC0)
        wtcC1 = prefetch_w(outPd, TILEW, 2, 16384, "wC")
        S.barrier()
    if stage_limit >= 5.5:
        stage_C(1, wtcC1)
        S.barrier()
    g3.close()

    g4 = ExitStack()
    dres = g4.enter_context(nc.sbuf_tensor("dres", [128, 65536], BF16))
    d_w1 = [wAB[:, 0:2048], wAB[:, 2048:4096]]
    d_w3 = [wAB[:, 4096:6144], wAB[:, 6144:8192]]
    d_h128 = g4.enter_context(nc.sbuf_tensor("d_h128", [128, 128], BF16))
    d_tn = g4.enter_context(nc.sbuf_tensor("d_tn", [128, TT], F32))
    d_u_ = g4.enter_context(nc.sbuf_tensor("d_u_", [128, TT], F32))
    d_hid = [g4.enter_context(nc.sbuf_tensor(f"d_hid{i}", [128, TT], BF16)) for i in range(2)]
    d_out = [g4.enter_context(nc.sbuf_tensor(f"d_out{i}", [128, TT], BF16)) for i in range(2)]
    pgq = [g4.enter_context(nc.psum_tensor(f"pgq{i}", [128, TT], F32)) for i in range(2)]
    plq = [g4.enter_context(nc.psum_tensor(f"plq{i}", [128, TT], F32)) for i in range(2)]
    phh = [g4.enter_context(nc.psum_tensor(f"phh{i}", [128, TT], F32)) for i in range(2)]

    if stage_limit >= 6:
        lh = S.load(d_h128[:, :], h128d, slot="h128")
        tl = None
        for ti in range(NTILE):
            for dt in range(ND):
                tl = S.loadT(dres[:, dt * CHUNK + ti * TT:dt * CHUNK + (ti + 1) * TT],
                             sp_lnx1[ti * TT:(ti + 1) * TT, dt * 128:(dt + 1) * 128], slot="dres")
        lnx1_ready = tl

        wd_ld = [None, None]
        pq_w = [None, None]
        pl_w = [None, None]
        ph_w = [None, None]
        do_w = [None, None]
        hid_w = [None, None]
        tn_w = None
        pend = [None]

        def emit_h128(ch, ti, hb, w_hid):
            pb2 = (ch * NTILE + ti) % 2
            w_h = S.mm(lambda pb2=pb2: phh[pb2][:, :],
                       [((lambda: d_h128[:, :]), (lambda hb=hb: d_hid[hb][:, :]))],
                       waits=[w_hid, lh if (ch == 0 and ti == 0) else None, ph_w[pb2]])
            hid_w[hb] = w_h
            ob = pb2
            w_ev = C(lambda c, pb2=pb2, ob=ob: c.activation(out=d_out[ob][:, :], in_=phh[pb2][:, :], func=AF.Copy),
                     waits=[w_h, do_w[ob]], inc=True)
            ph_w[pb2] = w_ev
            st = S.store(sp_hid.rearrange("p (a b) -> p a b", a=64)[:, ch, ti * TT:(ti + 1) * TT], d_out[ob][:, :], waits=[w_ev], slot=f"sd{ob}")
            do_w[ob] = st

        for ch in range(64):
            wb = ch % 2
            l1 = S.load(d_w1[wb][:, :], w1Td[:, ch * 2048:(ch + 1) * 2048],
                        waits=[wd_ld[wb], wAB_rel[0] if ch < 2 else None], slot=f"dw1{wb}")
            l3 = S.load(d_w3[wb][:, :], w3Td[:, ch * 2048:(ch + 1) * 2048],
                        waits=[wd_ld[wb], wAB_rel[0] if ch < 2 else None], slot=f"dw3{wb}")
            for ti in range(NTILE):
                pb = ti % 2
                w_g = S.mm(lambda pb=pb: pgq[pb][:, :],
                           [((lambda dt=dt, wb=wb: d_w1[wb][:, dt * 128:(dt + 1) * 128]),
                             (lambda dt=dt, ti=ti: dres[:, dt * CHUNK + ti * TT:dt * CHUNK + (ti + 1) * TT])) for dt in range(ND)],
                           waits=[l1, lnx1_ready if (ch == 0 and ti == 0) else None, pq_w[pb]])
                w_l = S.mm(lambda pb=pb: plq[pb][:, :],
                           [((lambda dt=dt, wb=wb: d_w3[wb][:, dt * 128:(dt + 1) * 128]),
                             (lambda dt=dt, ti=ti: dres[:, dt * CHUNK + ti * TT:dt * CHUNK + (ti + 1) * TT])) for dt in range(ND)],
                           waits=[l3, pl_w[pb]])
                w_tn = C(lambda c, pb=pb: c.activation(out=d_tn[:, :], in_=pgq[pb][:, :], func=AF.Tanh, scale=0.5),
                         waits=[w_g, tn_w], inc=True)
                w_u = V(lambda v, pb=pb: v.scalar_tensor_tensor(out=d_u_[:, :], in0=d_tn[:, :], scalar=1.0, in1=pgq[pb][:, :], op0=ALU.add, op1=ALU.mult),
                        waits=[w_tn], inc=True)
                pq_w[pb] = w_u
                tn_w = w_u
                w_hid = V(lambda v, pb=pb, hb=pb: v.scalar_tensor_tensor(out=d_hid[hb][:, :], in0=d_u_[:, :], scalar=0.5, in1=plq[pb][:, :], op0=ALU.mult, op1=ALU.mult),
                          waits=[w_l, hid_w[pb]], inc=True)
                pl_w[pb] = w_hid
                if pend[0] is not None:
                    emit_h128(*pend[0])
                pend[0] = (ch, ti, pb, w_hid)
            wd_ld[wb] = ("sTm", S.cur("sTm"))
        if pend[0] is not None:
            emit_h128(*pend[0])
        S.barrier()
    g4.close()

    g5 = ExitStack()
    eres_hi = g5.enter_context(nc.sbuf_tensor("eres_hi", [128, 32768], BF16))
    def eres(o):
        return wAB[:, o * 1024:(o + 1) * 1024] if o < 32 else eres_hi[:, (o - 32) * 1024:(o - 31) * 1024]
    e_hid = [g5.enter_context(nc.sbuf_tensor(f"e_hid{i}", [128, 8192], BF16)) for i in range(2)]
    e_sq = g5.enter_context(nc.sbuf_tensor("e_sq", [128, 1024], F32))
    e_st = g5.enter_context(nc.sbuf_tensor("e_st", [128, 2], F32))
    e_stp = g5.enter_context(nc.sbuf_tensor("e_stp", [128, 2], F32))
    e_f1 = g5.enter_context(nc.sbuf_tensor("e_f1", [128, 1024], BF16))
    e_x1 = g5.enter_context(nc.sbuf_tensor("e_x1", [128, 2048], F32))
    e_o = g5.enter_context(nc.sbuf_tensor("e_o", [128, 2048], F32))
    e_tt = g5.enter_context(nc.sbuf_tensor("e_tt", [128, 1024], F32))
    e_m = g5.enter_context(nc.sbuf_tensor("e_m", [128, 1], F32))
    e_v = g5.enter_context(nc.sbuf_tensor("e_v", [128, 1], F32))
    e_r = g5.enter_context(nc.sbuf_tensor("e_r", [128, 1], F32))
    e_w1 = g5.enter_context(nc.sbuf_tensor("e_w1", [128, 1], F32))
    e_mo = g5.enter_context(nc.sbuf_tensor("e_mo", [128, 1024], BF16))
    pe = [g5.enter_context(nc.psum_tensor(f"pe{i}", [128, 1024], F32)) for i in range(2)]

    NB = CHUNK // 128

    def stage_E(dh):
        wls = None
        for o in range(64):
            src = w2f[o:o + 1, :].rearrange("a (h d) -> (a h) d", h=128)[:, dh * 1024:(dh + 1) * 1024]
            wls = S.load(eres(o), src, slot="eres")
        pe_w = [[], []]
        hid_ld = [None, None]
        sq_w = None
        st_w = None
        f1_w = None
        o_w = None
        stp_w = None
        for tb in range(NB):
            hb = tb % 2
            lhd = S.load(e_hid[hb][:, :].rearrange("p (a b) -> p a b", a=64),
                         sp_hid.rearrange("p (a b) -> p a b", a=64)[:, :, tb * 128:(tb + 1) * 128],
                         waits=[hid_ld[hb]], slot=f"eh{hb}")
            pb = tb % 2
            pairs_a = [((lambda o=o, hb=hb: e_hid[hb][:, o * 128:(o + 1) * 128]),
                        (lambda o=o: eres(o)[:, 0:512])) for o in range(64)]
            pairs_b = [((lambda o=o, hb=hb: e_hid[hb][:, o * 128:(o + 1) * 128]),
                        (lambda o=o: eres(o)[:, 512:1024])) for o in range(64)]
            w_a = S.mm(lambda pb=pb: pe[pb][:, 0:512], pairs_a,
                       waits=[lhd, wls if tb == 0 else None] + (pe_w[pb] or []))
            w_b = S.mm(lambda pb=pb: pe[pb][:, 512:1024], pairs_b, waits=[])
            w_sq = C(lambda c, pb=pb: c.activation(out=e_sq[:, :], in_=pe[pb][:, :], func=AF.Square),
                     waits=[w_a, w_b, sq_w], inc=True)
            w_er1 = V(lambda v, pb=pb: v.tensor_reduce(out=e_st[:, 0:1], in_=pe[pb][:, :], axis=AX.X, op=ALU.add),
                      waits=[w_sq, st_w], chain=True)
            w_red = V(lambda v: v.tensor_reduce(out=e_st[:, 1:2], in_=e_sq[:, :], axis=AX.X, op=ALU.add),
                      inc=True, chain=True)
            sq_w = w_red
            if dh == 0:
                w_ev = C(lambda c, pb=pb: c.activation(out=e_f1[:, :], in_=pe[pb][:, :], func=AF.Copy),
                         waits=[f1_w, w_er1], inc=True)
                pe_w[pb] = [w_ev]
                f1_w = S.store(sp_ffn1[tb * 128:(tb + 1) * 128, :], e_f1[:, :], waits=[w_ev], slot="sf1")
                st_w = S.store(sp_st4[tb * 128:(tb + 1) * 128, :], e_st[:, :], waits=[w_red], slot="est4")
            else:
                lf = S.load(e_mo[:, :], sp_ffn1[tb * 128:(tb + 1) * 128, :], waits=[o_w], slot="emo")
                lp = S.load(e_stp[:, :], sp_st4[tb * 128:(tb + 1) * 128, :], waits=[stp_w], slot="estp")
                lx = S.load(e_x1[:, :], sp_x1[tb * 128:(tb + 1) * 128, :], waits=[o_w], slot="ex1")
                w_st = V(lambda v: v.tensor_tensor(out=e_st[:, :], in0=e_st[:, :], in1=e_stp[:, :], op=ALU.add),
                         waits=[lp, w_red], inc=True, chain=True)
                stp_w = w_st
                st_w = w_st
                V(lambda v: v.tensor_scalar(out=e_m[:, :], in0=e_st[:, 0:1], scalar1=1.0 / D_, scalar2=None, op0=ALU.mult), chain=True)
                V(lambda v: v.tensor_scalar(out=e_v[:, :], in0=e_st[:, 1:2], scalar1=1.0 / D_, scalar2=None, op0=ALU.mult), chain=True)
                V(lambda v: v.tensor_tensor(out=e_w1[:, :], in0=e_m[:, :], in1=e_m[:, :], op=ALU.mult), chain=True)
                V(lambda v: v.tensor_tensor(out=e_v[:, :], in0=e_v[:, :], in1=e_w1[:, :], op=ALU.subtract), chain=True)
                V(lambda v: v.tensor_scalar(out=e_v[:, :], in0=e_v[:, :], scalar1=EPS, scalar2=None, op0=ALU.add), chain=True)
                S.rsqrt(e_r[:, 0:1], e_w1[:, 0:1], e_v[:, 0:1])
                V(lambda v: v.tensor_scalar(out=e_tt[:, :], in0=e_mo[:, :], scalar1=e_m[:, 0:1], scalar2=e_r[:, 0:1], op0=ALU.subtract, op1=ALU.mult),
                  waits=[lf])
                V(lambda v: v.tensor_tensor(out=e_o[:, 0:1024], in0=e_tt[:, :], in1=e_x1[:, 0:1024], op=ALU.add),
                  waits=[lx, o_w])
                V(lambda v, pb=pb: v.tensor_scalar(out=e_tt[:, :], in0=pe[pb][:, :], scalar1=e_m[:, 0:1], scalar2=e_r[:, 0:1], op0=ALU.subtract, op1=ALU.mult))
                w_o = V(lambda v: v.tensor_tensor(out=e_o[:, 1024:2048], in0=e_tt[:, :], in1=e_x1[:, 1024:2048], op=ALU.add), inc=True)
                pe_w[pb] = [w_o]
                o_w = S.store(outd[tb * 128:(tb + 1) * 128, :], e_o[:, :], waits=[w_o], slot="out")
            hid_ld[hb] = ("sTm", S.cur("sTm"))

    if stage_limit >= 7:
        stage_E(0)
        S.barrier()
        stage_E(1)

    for sname in [s for s in S.cnt if s.startswith("S_")]:
        S.wait("sync", (sname, S.cur(sname)))
        S.wait("gpsimd", (sname, S.cur(sname)))

    block = es.enter_context(nc.Block())
    block.sync(lambda e: [f(e) for f in S.q["sync"]])
    block.gpsimd(lambda e: [f(e) for f in S.q["gpsimd"]])
    block.vector(lambda e: [f(e) for f in S.q["vector"]])
    block.scalar(lambda e: [f(e) for f in S.q["scalar"]])
    block.tensor(lambda e: [f(e) for f in S.q["tensor"]])
    g5.close()
    es.close()
    return nc



def _stage_inputs(inputs):
    x = np.asarray(inputs["L0_attn_hidden_states"], dtype=np.float32)
    inw = np.asarray(inputs["L0_attn_in_proj_w"], dtype=np.float32)
    gw = np.asarray(inputs["L0_attn_gate_proj_w"], dtype=np.float32)
    ow = np.asarray(inputs["L0_attn_out_proj_w"], dtype=np.float32)
    convw = np.asarray(inputs["L0_attn_conv_w"], dtype=np.float32)
    convb = np.asarray(inputs["L0_attn_conv_b"], dtype=np.float32)
    dtw = np.asarray(inputs["L0_attn_dt_w"], dtype=np.float32)
    dtb = np.asarray(inputs["L0_attn_dt_b"], dtype=np.float32)
    alog = np.asarray(inputs["L0_attn_A_log"], dtype=np.float32)
    bp = np.asarray(inputs["L0_attn_Bp"], dtype=np.float32)
    cp = np.asarray(inputs["L0_attn_Cp"], dtype=np.float32)
    dp = np.asarray(inputs["L0_attn_Dp"], dtype=np.float32)
    w1 = np.asarray(inputs["L0_ffn_W1"], dtype=np.float32)
    w2 = np.asarray(inputs["L0_ffn_W2"], dtype=np.float32)
    w3 = np.asarray(inputs["L0_ffn_W3"], dtype=np.float32)

    def tile_proj(w):
        wT = w.T.reshape(16, 128, 2, 16, 128)
        return np.ascontiguousarray(wT.transpose(1, 2, 3, 0, 4).reshape(128, 65536))

    def tile_ffn(w):
        wT = w.T.reshape(16, 128, 64, 128)
        return np.ascontiguousarray(wT.transpose(1, 2, 0, 3).reshape(128, 131072))

    oT = ow.T.reshape(32, 128, 2, 2, 512)
    outP_h = np.ascontiguousarray(oT.transpose(1, 2, 3, 0, 4).reshape(128, 65536))
    w2a = np.ascontiguousarray(w2.reshape(2048, 64, 128).transpose(1, 2, 0).reshape(64, 262144))

    dtw_h, bp_h, al_h, msk_h = [], [], [], []
    for h in range(2):
        dws = np.zeros((2048, 32), dtype=np.float32)
        bps = np.zeros((32, 2048), dtype=np.float32)
        als = np.zeros((32, 2048), dtype=np.float32)
        mss = np.zeros((32, 2048), dtype=np.float32)
        for hh in range(32):
            hg = 32 * h + hh
            base = (hg * 64) % 2048
            dws[base:base + 64, hh] = dtw[hg]
            bps[hh, base:base + 64] = bp[hg]
            als[hh, base:base + 64] = alog[hg]
            mss[hh, base:base + 64] = 1.0
        dtw_h.append(np.ascontiguousarray(dws.reshape(16, 128, 32).transpose(1, 0, 2).reshape(128, 512)))
        bp_h.append(np.ascontiguousarray(bps))
        al_h.append(np.ascontiguousarray(als))
        msk_h.append(np.ascontiguousarray(mss))

    shared = dict(
        inT=tile_proj(inw), gateT=tile_proj(gw), outP=outP_h,
        w1T=tile_ffn(w1), w3T=tile_ffn(w3), w2arr=w2a,
        convw=np.ascontiguousarray(convw.reshape(32, 128, 4).transpose(1, 0, 2).reshape(128, 128)),
        convb=np.ascontiguousarray(convb.reshape(32, 128).T),
        cp=np.ascontiguousarray(cp.reshape(-1).reshape(32, 128).T),
        dp=np.ascontiguousarray(dp.reshape(-1).reshape(32, 128).T),
        dtb=np.ascontiguousarray(dtb.reshape(2, 32).T),
        h64=np.ascontiguousarray(hadamard(64)),
        h128=np.ascontiguousarray(hadamard(128)),
        dtw0=dtw_h[0], dtw1=dtw_h[1], bp0=bp_h[0], bp1=bp_h[1],
        al0=al_h[0], al1=al_h[1], msk0=msk_h[0], msk1=msk_h[1])

    in_maps = []
    for c in range(NCORES):
        b, j = c // 4, c % 4
        s = j * CHUNK
        xe = np.zeros((4099, D_), dtype=np.float32)
        if j > 0:
            xe[:3] = x[b, s - 3:s]
        xe[3:] = x[b, s:s + CHUNK]
        xT = np.ascontiguousarray(xe.T.reshape(16, 128, 4099).transpose(1, 0, 2).reshape(128, 16 * 4099))
        mask = np.zeros((128, 8), dtype=np.float32)
        for r in range(4 * b, c):
            mask[:, r] = 1.0
        m = dict(shared)
        m["xT"] = xT
        m["x_tok"] = np.ascontiguousarray(x[b, s:s + CHUNK])
        m["chmask"] = mask
        in_maps.append(m)
    return in_maps


def kernel(**inputs):
    in_maps = _stage_inputs(inputs)
    nc = build_program()
    res = run_bass_kernel_spmd(nc, in_maps, core_ids=list(range(NCORES)))
    out = np.empty((B_, L_, D_), dtype=np.float32)
    for c in range(NCORES):
        b, j = c // 4, c % 4
        out[b, j * CHUNK:(j + 1) * CHUNK] = np.asarray(res.results[c]["out"])
    return out
